# version 30
# baseline (speedup 1.0000x reference)
import contextlib
import numpy as np
import ml_dtypes
import concourse.bass as bass
import concourse.mybir as mybir
from concourse.bass_utils import run_bass_kernel_spmd

F32 = mybir.dt.float32
BF16 = mybir.dt.bfloat16
AF = mybir.ActivationFunctionType
ALU = mybir.AluOpType
AX = mybir.AxisListType

ENGINES = ["sync", "scalar", "vector", "tensor", "gpsimd"]
COMPUTE_SEMS = {"ts": "tensor", "vs": "vector", "ss": "scalar", "gs": "gpsimd"}

D = 1024
T = 2048
NKC = 8
TT = 512
NTT = T // TT
NTI = 16
DK = 64
NE = 8
HALF_FC = 16
BIG = 1.0e30
EPS = 1.0e-5


class Builder:
    def __init__(self, nc):
        self.nc = nc
        self.ops = {e: [] for e in ENGINES}
        self.cnt = {}
        self.maxwait = {}
        self.sem_names = []

    def sem(self, name):
        if name not in self.cnt:
            self.cnt[name] = 0
            self.sem_names.append(name)
        return name

    def emit(self, eng, fn, waits=(), inc=None, inc_amt=1, hard=()):
        w = []
        for (soft, wl) in ((True, waits), (False, hard)):
            for x in wl:
                if x is None:
                    continue
                (s, v) = x
                if v <= 0:
                    continue
                key = (eng, s)
                if self.maxwait.get(key, -1) >= v:
                    continue
                self.maxwait[key] = v
                w.append((s, v))
        post = None
        if inc is not None:
            self.sem(inc)
            self.cnt[inc] += inc_amt
            post = (inc, self.cnt[inc])
        self.ops[eng].append((fn, w, inc, inc_amt))
        return post

    def dma(self, queue, out, in_, waits=(), chain=None):
        sem = "q_" + (chain if chain is not None else queue)
        self.sem(sem)
        prev = (sem, self.cnt[sem])
        return self.emit(queue, lambda e, o=out, i=in_: e.dma_start(o, i),
                         waits=list(waits) + [prev], inc=sem, inc_amt=16)

    def snap(self):
        return [(s, self.cnt.get(s, 0)) for s in ("ts", "vs", "ss", "gs")]

    def replay(self, eng_name, eng, sems):
        for (fn, waits, inc, amt) in self.ops[eng_name]:
            for (s, v) in waits:
                eng.wait_ge(sems[s], v)
            inst = fn(eng)
            if inc is not None:
                inst.then_inc(sems[inc], amt)

    def finalize(self, block):
        stack = contextlib.ExitStack()
        sems = {}
        for name in self.sem_names:
            sems[name] = stack.enter_context(self.nc.semaphore(name))
        for e in ENGINES:
            getattr(block, e)(lambda eng, n=e: self.replay(n, eng, sems))
        return stack


class Overlay:
    def __init__(self, nc, total_bytes):
        self.nc = nc
        self.total = total_bytes
        self.views = []

    def view(self, name, shape, dtype, off):
        per_part = int(np.prod(shape[1:])) * mybir.dt.size(dtype)
        assert off % 4 == 0 and off + per_part <= self.total, (
            name, off, per_part, self.total)
        h = self.nc.virtual_sbuf_tensor(name, shape, dtype)
        self.views.append((h, off))
        return h

    def allocate(self, stack, arena_name):
        arena = stack.enter_context(
            self.nc.sbuf_tensor(arena_name, [128, self.total], mybir.dt.uint8))
        base = self.nc.lookup_mloc(arena).addr
        for (h, off) in self.views:
            self.nc._tensor(h.name, h.shape, h.dtype, type="SB")
            mloc = self.nc.lookup_mloc(h)
            mloc.allocated = True
            mloc.addr = base + off


def make_consts():
    bf = ml_dtypes.bfloat16
    inv = 10000.0 ** (-np.arange(0, DK, 2) / DK)
    ang = np.outer(np.arange(T), inv)
    cos = np.repeat(np.cos(ang), 2, axis=1)
    sin = np.repeat(np.sin(ang), 2, axis=1)
    cos_fm = np.tile(cos.T, (2, 1)).astype(np.float32)
    sin_fm = np.tile(sin.T, (2, 1)).astype(np.float32)
    p = np.zeros((128, 128), np.float32)
    for j in range(64):
        p[2 * j + 1, 2 * j] = -1.0
        p[2 * j, 2 * j + 1] = 1.0
    ident = np.eye(128, dtype=np.float32)
    mask = np.zeros((128, 1024), np.float32)
    pidx = np.arange(128)
    bigp = 0.125 * BIG
    for c in range(384, 512):
        mask[:, c] = np.where((c - 384) <= pidx, 0.0, bigp)
    mask[:, 512:] = bigp
    iota8 = np.tile(np.arange(8, dtype=np.float32), (128, 1))
    c916 = np.tile(9.0 - np.arange(8, dtype=np.float32), (128, 1))
    sel = np.zeros((8, 8 * 128), np.float32)
    for e in range(8):
        sel[e, e * 128:(e + 1) * 128] = 1.0
    sel16 = np.zeros((16, 16 * 128), np.float32)
    for ti in range(16):
        sel16[ti, ti * 128:(ti + 1) * 128] = 1.0
    return {
        "c_cos": cos_fm.astype(bf), "c_sin": sin_fm.astype(bf),
        "c_p128": p.astype(bf), "c_ident": ident.astype(bf),
        "c_identf": ident, "c_mask": mask.astype(bf),
        "c_iota8": iota8, "c_c916": c916, "c_sel": sel.astype(bf),
        "c_sel16": sel16.astype(bf),
    }


def build_program(debug=False):
    nc = bass.Bass("TRN2", target_bir_lowering=False)
    b = Builder(nc)

    dx = nc.dram_tensor("L0_attn_x", [T, D], F32, kind="ExternalInput")
    dwq = nc.dram_tensor("L0_attn_Wq", [D, D], F32, kind="ExternalInput")
    dwk = nc.dram_tensor("L0_attn_Wk", [D, D], F32, kind="ExternalInput")
    dwv = nc.dram_tensor("L0_attn_Wv", [D, D], F32, kind="ExternalInput")
    dwo = nc.dram_tensor("L0_attn_Wo", [D, D], F32, kind="ExternalInput")
    dcent = nc.dram_tensor("L0_ffn_centroids", [NE, D], F32,
                           kind="ExternalInput")
    dw1 = nc.dram_tensor("L0_ffn_w1", [NE, D, 4096], F32,
                         kind="ExternalInput")
    db1 = nc.dram_tensor("L0_ffn_b1", [NE, 4096], F32, kind="ExternalInput")
    dw2 = nc.dram_tensor("L0_ffn_w2", [NE, 4096, D], F32,
                         kind="ExternalInput")
    db2 = nc.dram_tensor("L0_ffn_b2", [NE, D], F32, kind="ExternalInput")
    dgam = nc.dram_tensor("L0_ffn_gamma", [NE, D], F32, kind="ExternalInput")
    dbet = nc.dram_tensor("L0_ffn_beta", [NE, D], F32, kind="ExternalInput")
    dout = nc.dram_tensor("out", [T, D], F32, kind="ExternalOutput")
    drows = nc.dram_tensor("rows_scratch", [2, T], F32)
    dbg = {}
    if debug:
        for (n, shp, dt) in [("dbg_acc_p1", [128, NKC * T], F32),
                             ("dbg_xnT", [128, NKC * T], BF16),
                             ("dbg_stat", [128, 84], F32),
                             ("dbg_acc_p2", [128, NKC * T], F32),
                             ("dbg_xhatT", [128, NKC * T], BF16),
                             ("dbg_wem", [8, T], BF16),
                             ("dbg_maskem", [8, T], F32),
                             ("dbg_alpha", [8, T], F32),
                             ("dbg_qb", [128, T], BF16),
                             ("dbg_w1h", [128, NKC * HALF_FC * 128], BF16),
                             ("dbg_w2h", [128, HALF_FC * D], BF16),
                             ("dbg_tbuf", [128, HALF_FC * TT], BF16),
                             ("dbg_hbuf", [128, NKC * TT], BF16),
                             ("dbg_wbcb", [128, T], BF16),
                             ("dbg_b1t", [128, 32], F32),
                             ("dbg_gamT", [128, 8], F32),
                             ("dbg_betT", [128, 8], F32),
                             ("dbg_kb", [128, T], BF16),
                             ("dbg_vb", [128, T], BF16),
                             ("dbg_oT", [128, T], BF16)]:
            dbg[n] = nc.dram_tensor(n, shp, dt, kind="ExternalOutput")
    dc = {}
    for (n, shp, dt) in [("c_cos", [128, T], BF16), ("c_sin", [128, T], BF16),
                         ("c_p128", [128, 128], BF16),
                         ("c_ident", [128, 128], BF16),
                         ("c_identf", [128, 128], F32),
                         ("c_mask", [128, 1024], BF16),
                         ("c_iota8", [128, 8], F32),
                         ("c_c916", [128, 8], F32),
                         ("c_sel", [8, 8 * 128], BF16),
                         ("c_sel16", [16, 16 * 128], BF16)]:
        dc[n] = nc.dram_tensor(n, shp, dt, kind="ExternalInput")

    al = contextlib.ExitStack()

    acc = al.enter_context(nc.sbuf_tensor("acc", [128, NKC * T], F32))
    xhatT = al.enter_context(nc.sbuf_tensor("xhatT", [128, NKC * T], BF16))
    OVC = Overlay(nc, 1792)
    identb = OVC.view("identb", [128, 128], BF16, 0)
    identf = OVC.view("identf", [128, 128], F32, 512)
    iota8 = OVC.view("iota8", [128, 8], F32, 1024)
    c916 = OVC.view("c916", [128, 8], F32, 1056)
    onescol = OVC.view("onescol", [1, 128], BF16, 1088)
    stat = OVC.view("stat", [128, 84], F32, 1344)

    OVB = Overlay(nc, 32768)
    xnT = OVB.view("xnT", [128, NKC * T], BF16, 0)
    w1q0 = OVB.view("w1q0", [128, NKC * 8 * 128], BF16, 0)
    w1q1 = OVB.view("w1q1", [128, NKC * 8 * 128], BF16, 16384)

    OV = Overlay(nc, 78208)
    xstage = OV.view("xstage", [128, 2 * 1024], F32, 0)
    xnf = OV.view("xnf", [128, 1024], F32, 8192)
    xnb = OV.view("xnb", [128, 1024], BF16, 12288)
    xnf2 = OV.view("xnf2", [128, 1024], F32, 38912)
    xnb2 = OV.view("xnb2", [128, 1024], BF16, 43008)
    qb = OV.view("qb", [128, T], BF16, 0)
    kb = OV.view("kb", [128, T], BF16, 4096)
    vb = OV.view("vb", [128, T], BF16, 8192)
    qtmp = OV.view("qtmp", [128, 2 * TT], BF16, 12288)
    wqs = OV.view("wqs", [128, NKC * 128], BF16, 14336)
    wks = OV.view("wks", [128, NKC * 128], BF16, 16384)
    wvs = OV.view("wvs", [128, NKC * 128], BF16, 18432)
    wos = OV.view("wos", [128, 1024], BF16, 20480)
    S_p0 = OV.view("S_p0", [128, 2 * T], BF16, 22528)
    S_p1 = OV.view("S_p1", [128, 2 * T], BF16, 30720)
    A_sb = OV.view("A_sb", [128, 2 * T], BF16, 38912)
    AT_sb = OV.view("AT_sb", [128, T], BF16, 47104)
    oT = OV.view("oT", [128, T], BF16, 51200)
    cosb = OV.view("cosb", [128, T], BF16, 55296)
    sinb = OV.view("sinb", [128, T], BF16, 59392)
    maskm = OV.view("maskm", [128, 1024], BF16, 63488)
    p128 = OV.view("p128b", [128, 128], BF16, 65536)
    affem = OV.view("affem", [8, T], F32, 22528)
    afftm = OV.view("afftm", [128, 128], F32, 30720)
    ismax = OV.view("ismax", [128, 128], F32, 31232)
    maskem = OV.view("maskem", [8, T], F32, 31744)
    mu2bc = OV.view("mu2bc", [128, T], F32, 39936)
    r2bc = OV.view("r2bc", [128, T], F32, 48128)
    xh_tmp = OV.view("xh_tmp", [128, T], BF16, 56320)
    cent32 = OV.view("cent32", [8, 1024], F32, 60416)
    centb = OV.view("centb", [8, 1024], BF16, 64512)
    centT = OV.view("centT", [128, 64], BF16, 66560)
    trsc = OV.view("trsc", [32, 128], F32, 66688)
    rowsb = OV.view("rowsb", [1, 4096], BF16, 30720)
    w2q0 = OV.view("w2q0", [128, 8 * D], BF16, 0)
    w2q1 = OV.view("w2q1", [128, 8 * D], BF16, 16384)
    tbufq = OV.view("tbufq", [128, 8 * TT], BF16, 32768)
    hbuf0 = OV.view("hbuf0", [128, NKC * TT], BF16, 40960)
    hbuf1 = OV.view("hbuf1", [128, NKC * TT], BF16, 49152)
    ztmp = OV.view("ztmp", [128, TT], F32, 57344)
    wbcb = OV.view("wbcb", [128, T], BF16, 59392)
    w_em = OV.view("w_em", [8, T], BF16, 63488)
    b2all = OV.view("b2all", [8, 1024], BF16, 67584)
    pb1a = OV.view("pb1a", [32, 128], F32, 71680)
    pb1b = OV.view("pb1b", [32, 128], F32, 72192)
    pgma = OV.view("pgma", [8, 128], F32, 72704)
    pgmb = OV.view("pgmb", [8, 128], F32, 73216)
    pbta = OV.view("pbta", [8, 128], F32, 73728)
    pbtb = OV.view("pbtb", [8, 128], F32, 74240)
    b1t0 = OV.view("b1t0", [128, 32], F32, 74752)
    b1t1 = OV.view("b1t1", [128, 32], F32, 74880)
    gamT0 = OV.view("gamT0", [128, 8], F32, 75008)
    gamT1 = OV.view("gamT1", [128, 8], F32, 75040)
    betT0 = OV.view("betT0", [128, 8], F32, 75072)
    betT1 = OV.view("betT1", [128, 8], F32, 75104)
    selb = OV.view("selb", [8, 8 * 128], BF16, 76160)
    ltrb = OV.view("ltrb", [16, 256], BF16, 12288)
    lbc = OV.view("lbc", [128, T], BF16, 65792)
    trscb = OV.view("trscb", [16, 256], BF16, 60416)
    sel16 = OV.view("sel16", [16, 16 * 128], BF16, 69888)
    out_tm0 = OV.view("out_tm0", [128, 1024], F32, 0)
    out_tm1 = OV.view("out_tm1", [128, 1024], F32, 4096)

    P = [al.enter_context(nc.psum_tensor(f"P{i}", [128, 512], F32))
         for i in range(6)]
    PT0 = al.enter_context(nc.psum_tensor("PT0", [128, 1024], BF16))
    PT1 = al.enter_context(nc.psum_tensor("PT1", [128, 1024], BF16))
    PTs = [PT0, PT1]

    ovstack = contextlib.ExitStack()
    OVC.allocate(ovstack, "arenaC")
    OVB.allocate(ovstack, "arenaB")
    OV.allocate(ovstack, "arenaP")

    c_loads = []
    for (dst, src) in [(identb[:], dc["c_ident"][:, :]),
                       (identf[:], dc["c_identf"][:, :]),
                       (p128[:], dc["c_p128"][:, :]),
                       (iota8[:], dc["c_iota8"][:, :]),
                       (c916[:], dc["c_c916"][:, :]),
                       (maskm[:], dc["c_mask"][:, :]),
                       (cosb[:], dc["c_cos"][:, :]),
                       (sinb[:], dc["c_sin"][:, :]),
                       (sel16[:], dc["c_sel16"][:, :])]:
        c_loads.append(b.dma("gpsimd", dst, src, chain="cg"))
    ocs = b.emit("vector", lambda e: e.memset(onescol[:], 1.0), inc="vs")
    consts_done = c_loads[-1]

    xs = [xstage[:, 0:1024], xstage[:, 1024:2048]]
    xnfs = [xnf, xnf2]
    xnbs = [xnb, xnb2]
    psA = [P[4], P[5]]
    ps_busy = {}
    pt_busy = {}
    par_done = [None, None]
    xnb_free = [None, None]
    ln_loads = [b.dma("gpsimd", xs[ti % 2],
                      dx.rearrange("(ti p) c -> ti p c", p=128)[ti],
                      waits=[consts_done], chain="x%d" % (ti % 2))
                for ti in range(2)]
    for ti in range(NTI):
        pi2 = ti % 2
        sb = 48 * pi2
        xbuf = xs[pi2]
        xnf_t = xnfs[pi2]
        xnb_t = xnbs[pi2]
        w0 = [ln_loads[ti], par_done[pi2]]
        pv = [None]

        def vch(fn, waits=()):
            pv[0] = b.emit("vector", fn, waits=waits, hard=[pv[0]], inc="vs")
            return pv[0]

        vch(lambda e, xbuf=xbuf, sb=sb: e.reduce_sum(
            stat[:, sb + 0:sb + 1], xbuf, AX.X), waits=w0)
        vch(lambda e, sb=sb: e.tensor_scalar(
            stat[:, sb + 1:sb + 2], stat[:, sb + 0:sb + 1], 1.0 / D, None,
            ALU.mult))
        sq = b.emit("scalar", lambda e, xbuf=xbuf, sb=sb, xnf_t=xnf_t:
                    e.activation(
                        xnf_t[:], xbuf, AF.Square,
                        accum_out=stat[:, sb + 2:sb + 3]),
                    waits=w0, inc="ss")
        vch(lambda e, sb=sb: e.tensor_scalar(
            stat[:, sb + 3:sb + 4], stat[:, sb + 2:sb + 3], 1.0 / D, None,
            ALU.mult), waits=[sq])
        vch(lambda e, sb=sb: e.tensor_tensor(
            stat[:, sb + 4:sb + 5], stat[:, sb + 1:sb + 2],
            stat[:, sb + 1:sb + 2], ALU.mult))
        vch(lambda e, sb=sb: e.tensor_tensor(
            stat[:, sb + 4:sb + 5], stat[:, sb + 3:sb + 4],
            stat[:, sb + 4:sb + 5], ALU.subtract))
        ve = vch(lambda e, sb=sb: e.tensor_scalar(
            stat[:, sb + 5:sb + 6], stat[:, sb + 4:sb + 5], EPS, None,
            ALU.add))
        sz = b.emit("scalar", lambda e, sb=sb: e.activation(
            stat[:, sb + 6:sb + 7], stat[:, sb + 5:sb + 6], AF.Sqrt),
            waits=[ve], inc="ss")
        vch(lambda e, sb=sb: e.reciprocal(
            stat[:, sb + 7:sb + 8], stat[:, sb + 6:sb + 7]), waits=[sz])
        xnfo = vch(lambda e, xbuf=xbuf, sb=sb, xnf_t=xnf_t: e.tensor_scalar(
            xnf_t[:], xbuf, stat[:, sb + 1:sb + 2], stat[:, sb + 7:sb + 8],
            ALU.subtract, ALU.mult), waits=[sq])
        vch(lambda e, sb=sb, xnf_t=xnf_t: e.reduce_sum(
            stat[:, sb + 8:sb + 9], xnf_t[:], AX.X))
        vch(lambda e, ti=ti, sb=sb: e.tensor_scalar(
            stat[:, 16 + ti:17 + ti], stat[:, sb + 8:sb + 9], 1.0 / D, None,
            ALU.mult))
        sq2 = b.emit("scalar", lambda e, sb=sb, xnf_t=xnf_t, xnb_t=xnb_t:
                     e.activation(
                         xnb_t[:], xnf_t[:], AF.Square,
                         accum_out=stat[:, sb + 9:sb + 10]),
                     waits=[xnfo, xnb_free[pi2]], inc="ss")
        vch(lambda e, sb=sb: e.tensor_scalar(
            stat[:, sb + 10:sb + 11], stat[:, sb + 9:sb + 10], 1.0 / D, None,
            ALU.mult), waits=[sq2])
        vch(lambda e, ti=ti, sb=sb: e.tensor_tensor(
            stat[:, sb + 11:sb + 12], stat[:, 16 + ti:17 + ti],
            stat[:, 16 + ti:17 + ti], ALU.mult))
        vch(lambda e, sb=sb: e.tensor_tensor(
            stat[:, sb + 11:sb + 12], stat[:, sb + 10:sb + 11],
            stat[:, sb + 11:sb + 12], ALU.subtract))
        v2 = vch(lambda e, sb=sb: e.tensor_scalar(
            stat[:, sb + 12:sb + 13], stat[:, sb + 11:sb + 12], EPS, None,
            ALU.add))
        sz2 = b.emit("scalar", lambda e, sb=sb: e.activation(
            stat[:, sb + 13:sb + 14], stat[:, sb + 12:sb + 13], AF.Sqrt),
            waits=[v2], inc="ss")
        vch(lambda e, ti=ti, sb=sb: e.reciprocal(
            stat[:, 32 + ti:33 + ti], stat[:, sb + 13:sb + 14]),
            waits=[sz2])
        xnbo = vch(lambda e, xnf_t=xnf_t, xnb_t=xnb_t: e.tensor_copy(
            xnb_t[:], xnf_t[:]), waits=[sq2])
        vch(lambda e, sb=sb: e.tensor_scalar(
            stat[:, sb + 14:sb + 15], stat[:, sb + 7:sb + 8], 1.0, None,
            ALU.add))
        vch(lambda e, sb=sb: e.tensor_tensor(
            stat[:, sb + 15:sb + 16], stat[:, sb + 1:sb + 2],
            stat[:, sb + 7:sb + 8], ALU.mult))
        uo = vch(lambda e, xbuf=xbuf, sb=sb: e.tensor_scalar(
            xbuf, xbuf, stat[:, sb + 14:sb + 15], stat[:, sb + 15:sb + 16],
            ALU.mult, ALU.subtract))
        for half in range(2):
            pb = psA[half]
            tr = None
            for j in range(4):
                kc = half * 4 + j
                tr = b.emit("tensor", lambda e, kc=kc, j=j, pb=pb, xbuf=xbuf:
                            e.transpose(pb[:, j * 128:(j + 1) * 128],
                                        xbuf[:, kc * 128:(kc + 1) * 128],
                                        identf[:]),
                            waits=[uo, consts_done, ps_busy.get(id(pb))],
                            inc="ts")
            ev = b.emit("vector", lambda e, half=half, ti=ti, pb=pb:
                        e.tensor_copy(
                            acc[:].rearrange("p (kc t) -> p kc t", kc=NKC)
                            [:, half * 4:(half + 1) * 4,
                             ti * 128:(ti + 1) * 128], pb[:]),
                        waits=[tr], inc="vs")
            ps_busy[id(pb)] = ev
        pt = PTs[ti % 2]
        tr = None
        for kc in range(NKC):
            tr = b.emit("tensor", lambda e, kc=kc, pt=pt, xnb_t=xnb_t:
                        e.transpose(
                            pt[:, kc * 128:(kc + 1) * 128],
                            xnb_t[:, kc * 128:(kc + 1) * 128], identb[:]),
                        waits=[xnbo, pt_busy.get(id(pt))], inc="ts")
        xnb_free[pi2] = ("ts", b.cnt["ts"])
        ev = b.emit("vector", lambda e, ti=ti, pt=pt: e.tensor_copy(
            xnT[:].rearrange("p (kc t) -> p kc t", kc=NKC)
            [:, :, ti * 128:(ti + 1) * 128], pt[:]),
            waits=[tr], inc="vs")
        pt_busy[id(pt)] = ev
        par_done[pi2] = pv[0]
        if ti + 2 < NTI:
            ln_loads.append(b.dma(
                "gpsimd", xs[pi2],
                dx.rearrange("(ti p) c -> ti p c", p=128)[ti + 2],
                waits=[("ts", b.cnt["ts"])], chain="x%d" % (ti % 2)))
    p1_done = b.snap()
    if debug:
        b.dma("gpsimd", dbg["dbg_acc_p1"][:, :], acc[:], waits=p1_done)
        b.dma("gpsimd", dbg["dbg_xnT"][:, :], xnT[:], waits=p1_done)
        b.dma("gpsimd", dbg["dbg_stat"][:, :], stat[:, :], waits=p1_done)

    sp_busy = {}
    avp_busy = None
    wo_busy = {}
    wslice_free = {"qkv": None, "o": None}
    wcast_prev = None
    last_acc_add = None
    epi_mark = [None]
    lsc_prev = [None]
    zps_busy_dummy = {}

    def stream_wslice(ch, first):
        nonlocal wcast_prev
        outs = {}
        for (nm, wsb, dwt, kind) in [("q", wqs, dwq, "col"),
                                     ("k", wks, dwk, "col"),
                                     ("v", wvs, dwv, "col"),
                                     ("o", wos, dwo, "row")]:
            waits = [wslice_free["o" if nm == "o" else "qkv"]]
            if first:
                waits.append(consts_done)
            if kind == "col":
                src_ = dwt.rearrange("(kc p) c -> p kc c", p=128)[
                    :, :, ch * 128:(ch + 1) * 128]
                dst = wsb[:].rearrange("p (kc c) -> p kc c", kc=NKC)
            else:
                src_ = dwt.rearrange("(c p) m -> p c m", p=128)[:, ch, :]
                dst = wsb[:, 0:1024]
            outs[nm] = b.dma("gpsimd", dst, src_, waits=waits,
                             chain="ws_" + nm)
        wcast_prev = outs
        return outs

    for ch in range(NKC):
        wst = stream_wslice(ch, ch == 0)
        projp = [P[3], P[4]]
        pe_busy2 = {}
        qk_last = None
        rope_prev = {}
        for (wsb, dstb, wkey) in [(wqs, qb, "q"), (wks, kb, "k")]:
            for tt in range(NTT):
                pb = projp[tt % 2]
                w = p1_done + [wst[wkey], pe_busy2.get(id(pb)),
                               sp_busy.get(id(pb)), wo_busy.get(id(pb)),
                               ps_busy.get(id(pb))]
                mm = None
                for kc in range(NKC):
                    mm = b.emit("tensor", lambda e, kc=kc, tt=tt, wsb=wsb,
                                pb=pb: e.matmul(
                                    pb[:], wsb[:, kc * 128:(kc + 1) * 128],
                                    xnT[:, kc * T + tt * TT:
                                        kc * T + tt * TT + TT],
                                    start=(kc == 0), stop=(kc == NKC - 1)),
                                waits=w, inc="ts")
                sl = (tt % 2) * TT
                ev = b.emit("vector", lambda e, pb=pb, sl=sl: e.tensor_copy(
                    qtmp[:, sl:sl + TT], pb[:]), waits=[mm, epi_mark[0]],
                    hard=[rope_prev.get(sl)], inc="vs")
                rot = b.emit("tensor", lambda e, sl=sl, pb=pb: e.matmul(
                    pb[:], p128[:], qtmp[:, sl:sl + TT], start=True,
                    stop=True), waits=[ev], inc="ts")
                qc = b.emit("vector", lambda e, sl=sl, tt=tt, dstb=dstb:
                            e.tensor_tensor(
                                dstb[:, tt * TT:(tt + 1) * TT],
                                qtmp[:, sl:sl + TT],
                                cosb[:, tt * TT:(tt + 1) * TT], ALU.mult),
                            hard=[ev], inc="vs")
                rs = b.emit("vector", lambda e, sl=sl, tt=tt, pb=pb:
                            e.tensor_tensor(
                                qtmp[:, sl:sl + TT], pb[:],
                                sinb[:, tt * TT:(tt + 1) * TT], ALU.mult),
                            waits=[rot], hard=[qc], inc="vs")
                fin = b.emit("vector", lambda e, sl=sl, tt=tt, dstb=dstb:
                             e.tensor_tensor(
                                 dstb[:, tt * TT:(tt + 1) * TT],
                                 dstb[:, tt * TT:(tt + 1) * TT],
                                 qtmp[:, sl:sl + TT], ALU.add),
                             hard=[rs, qc], inc="vs")
                pe_busy2[id(pb)] = rs
                rope_prev[sl] = fin
                qk_last = fin
        v_done = None
        for tig in range(4):
            pb = projp[tig % 2]
            w = p1_done + [wst["v"], pe_busy2.get(id(pb))]
            mm = None
            for tl in range(4):
                ti = tig * 4 + tl
                for kc in range(NKC):
                    mm = b.emit("tensor", lambda e, kc=kc, ti=ti, tl=tl,
                                pb=pb: e.matmul(
                                    pb[:, tl * 128:(tl + 1) * 128],
                                    xnT[:, kc * T + ti * 128:
                                        kc * T + ti * 128 + 128],
                                    wvs[:, kc * 128:(kc + 1) * 128],
                                    start=(kc == 0), stop=(kc == NKC - 1)),
                                waits=w, inc="ts")
            ev = b.emit("vector", lambda e, tig=tig, pb=pb: e.tensor_copy(
                vb[:, tig * 512:(tig + 1) * 512], pb[:]), waits=[mm],
                inc="vs")
            pe_busy2[id(pb)] = ev
            v_done = ev
        wslice_free["qkv"] = ("ts", b.cnt["ts"])
        nm_prev = {}
        transp_end = {0: None, 1: None}
        prev_i = None

        def emit_transp_av(pi):
            nonlocal avp_busy, mmv_last
            mmv = None
            for h in range(2):
                hoff = h * T
                nb = pi + 1
                atl = None
                tr = None
                for b0 in range(0, nb, 8):
                    bn = min(8, nb - b0)
                    pt = PTs[(b0 // 8) % 2]
                    tr = None
                    Spi = S_p0 if (pi % 2 == 0) else S_p1
                    for bb in range(bn):
                        blk = b0 + bb
                        tr = b.emit(
                            "tensor",
                            lambda e, hoff=hoff, blk=blk, bb=bb, pt=pt,
                            Spi=Spi: e.transpose(
                                pt[:, bb * 128:(bb + 1) * 128],
                                Spi[:, hoff + blk * 128:
                                    hoff + blk * 128 + 128], identb[:]),
                            waits=[nm_prev[h], pt_busy.get(id(pt))],
                            inc="ts")
                    aev = b.emit("vector",
                                 lambda e, b0=b0, bn=bn, pt=pt: e.tensor_copy(
                                     AT_sb[:, b0 * 128:(b0 + bn) * 128],
                                     pt[:, 0:bn * 128]),
                                 waits=[tr], inc="vs")
                    pt_busy[id(pt)] = aev
                    atl = aev
                transp_end[h] = ("ts", b.cnt["ts"])
                for blk in range(pi + 1):
                    wts = [atl, v_done]
                    if blk == 0 and h == 0:
                        wts.append(avp_busy)
                    mmv = b.emit(
                        "tensor",
                        lambda e, h=h, blk=blk, pi=pi: e.matmul(
                            P[5][h * 64:(h + 1) * 64, 0:128],
                            vb[:, blk * 128 + h * 64:
                               blk * 128 + h * 64 + 64],
                            AT_sb[:, blk * 128:(blk + 1) * 128],
                            start=(blk == 0), stop=(blk == pi),
                            tile_position=(0, h * 64)),
                        waits=wts, inc="ts")
            oev = b.emit("scalar", lambda e, pi=pi: e.copy(
                oT[:, pi * 128:(pi + 1) * 128], P[5][:, 0:128]),
                waits=[mmv, lsc_prev[0]], inc="ss")
            avp_busy = oev

        mmv_last = None

        def emit_smm_span0(i):
            W = (i + 1) * 128
            wdt = min(512, W)
            bankA = P[i % 2]
            bankB = P[2 + (i % 2)]
            mm0 = {}
            for (h, base, pb) in [(0, 0, bankA), (1, 64, bankB)]:
                wts = [qk_last, sp_busy.get(id(pb)),
                       pe_busy2.get(id(pb)), wo_busy.get(id(pb))]
                mm0[h] = b.emit(
                    "tensor",
                    lambda e, i=i, wdt=wdt, base=base, pb=pb:
                    e.matmul(pb[:, 0:wdt],
                             qb[base:base + 64, i * 128:(i + 1) * 128],
                             kb[base:base + 64, 0:wdt],
                             start=True, stop=True),
                    waits=wts, inc="ts")
            return mm0

        def emit_smm_evacs(i, ex_mark, mm0):
            W = (i + 1) * 128
            nspan = (i // 4) + 1
            Ssb = S_p0 if (i % 2 == 0) else S_p1
            bankA = P[i % 2]
            bankB = P[2 + (i % 2)]
            for si in range(nspan):
                s0 = si * 512
                wdt = min(512, W - s0)
                if si == 0:
                    mmAB = mm0
                else:
                    mmAB = {}
                    for (h, base, pb) in [(0, 0, bankA), (1, 64, bankB)]:
                        wts = [qk_last, sp_busy.get(id(pb)),
                               pe_busy2.get(id(pb)), wo_busy.get(id(pb))]
                        mmAB[h] = b.emit(
                            "tensor",
                            lambda e, i=i, s0=s0, wdt=wdt, base=base, pb=pb:
                            e.matmul(pb[:, 0:wdt],
                                     qb[base:base + 64,
                                        i * 128:(i + 1) * 128],
                                     kb[base:base + 64, s0:s0 + wdt],
                                     start=True, stop=True),
                            waits=wts, inc="ts")
                for (h, pb) in [(0, bankA), (1, bankB)]:
                    hoff = h * T
                    if si == nspan - 1:
                        r = i % 4
                        off = 384 - r * 128
                        ev = b.emit(
                            "vector",
                            lambda e, hoff=hoff, s0=s0, wdt=wdt, off=off,
                            pb=pb, Ssb=Ssb: e.scalar_tensor_tensor(
                                Ssb[:, hoff + s0: hoff + s0 + wdt],
                                pb[:, 0:wdt], -0.125,
                                maskm[:, off:off + wdt],
                                ALU.mult, ALU.add),
                            waits=[mmAB[h]] + list(ex_mark or []), inc="vs")
                    else:
                        ev = b.emit(
                            "scalar",
                            lambda e, hoff=hoff, s0=s0, wdt=wdt, pb=pb,
                            Ssb=Ssb: e.activation(
                                Ssb[:, hoff + s0: hoff + s0 + wdt],
                                pb[:, 0:wdt], AF.Copy, scale=-0.125),
                            waits=[mmAB[h]] + list(ex_mark or []), inc="ss")
                    sp_busy[id(pb)] = ev
            sev_marks[i] = ("ss", b.cnt["ss"])

        sev_marks = {}
        emit_smm_evacs(0, None, emit_smm_span0(0))
        tr_m_prev = None
        mm0_next = None
        for i in range(NTI):
            W = (i + 1) * 128
            if i + 1 < NTI:
                mm0_next = emit_smm_span0(i + 1)
            if prev_i is not None:
                emit_transp_av(prev_i)
            Ssb_i = S_p0 if (i % 2 == 0) else S_p1
            ex = None
            for h in range(2):
                hoff = h * T
                lslot = (0 if h == 0 else 48) + i
                ex = b.emit(
                    "scalar", lambda e, hoff=hoff, W=W, h=h, lslot=lslot,
                    Ssb_i=Ssb_i: e.activation(
                        Ssb_i[:, hoff:hoff + W], Ssb_i[:, hoff:hoff + W],
                        AF.Exp, bias=0.0, scale=-1.0,
                        accum_out=stat[:, lslot:lslot + 1]),
                    waits=[("vs", b.cnt["vs"]), sev_marks.get(i)],
                    inc="ss")
                nm_prev[h] = ex
            if i + 1 < NTI:
                emit_smm_evacs(i + 1, tr_m_prev, mm0_next)
            tr_m_prev = [transp_end[0], transp_end[1]]
            prev_i = i
        emit_transp_av(prev_i)
        r1_ = b.emit("vector", lambda e: e.reciprocal(
            stat[:, 0:16], stat[:, 0:16]),
            waits=[nm_prev[0], nm_prev[1]], inc="vs")
        r2x = b.emit("vector", lambda e: e.reciprocal(
            stat[:, 48:64], stat[:, 48:64]), hard=[r1_], inc="vs")
        lt1 = b.emit("tensor", lambda e: e.transpose(
            P[4][0:16, 0:128], stat[:, 0:16], identf[:]),
            waits=[r2x, wo_busy.get(id(P[4])), pe_busy2.get(id(P[4])),
                   sp_busy.get(id(P[4]))], inc="ts")
        lt2 = b.emit("tensor", lambda e: e.transpose(
            P[5][0:16, 0:128], stat[:, 48:64], identf[:]),
            waits=[avp_busy, sp_busy.get(id(P[5]))], inc="ts")
        le1 = b.emit("vector", lambda e: e.tensor_copy(
            ltrb[0:16, 0:128], P[4][0:16, 0:128]), waits=[lt2], inc="vs")
        le2 = b.emit("vector", lambda e: e.tensor_copy(
            ltrb[0:16, 128:256], P[5][0:16, 0:128]), hard=[le1], inc="vs")
        lev = None
        for ti in range(NTI):
            pb = [P[0], P[1], P[2], P[3]][ti % 4]
            lmm = None
            for h in range(2):
                lmm = b.emit(
                    "tensor", lambda e, ti=ti, h=h, pb=pb: e.matmul(
                        pb[h * 64:(h + 1) * 64, 0:128],
                        sel16[0:16, ti * 128:ti * 128 + 64],
                        ltrb[0:16, h * 128:(h + 1) * 128],
                        start=True, stop=True, tile_position=(0, h * 64)),
                    waits=[le2, consts_done, sp_busy.get(id(pb))], inc="ts")
            lev = b.emit("vector", lambda e, ti=ti, pb=pb: e.tensor_copy(
                lbc[:, ti * 128:(ti + 1) * 128], pb[:, 0:128]),
                waits=[lmm], inc="vs")
            sp_busy[id(pb)] = lev
        epi_mark[0] = ("ts", b.cnt["ts"])
        lsc = b.emit("vector", lambda e: e.tensor_tensor(
            oT[:], oT[:], lbc[:], ALU.mult),
            waits=[("ss", b.cnt["ss"])], hard=[lev], inc="vs")
        lsc_prev[0] = lsc
        wop = [P[3], P[4]]
        for mc in range(NKC):
            for tt in range(NTT):
                pb = wop[(mc * NTT + tt) % 2]
                wts = [avp_busy, wst["o"], wo_busy.get(id(pb)),
                       pe_busy2.get(id(pb)), sp_busy.get(id(pb)), lsc]
                mm = b.emit("tensor", lambda e, mc=mc, tt=tt, pb=pb: e.matmul(
                    pb[:], wos[:, mc * 128:(mc + 1) * 128],
                    oT[:, tt * TT:(tt + 1) * TT], start=True, stop=True),
                    waits=wts, inc="ts")
                ad = b.emit("vector", lambda e, mc=mc, tt=tt, pb=pb:
                            e.tensor_tensor(
                                acc[:, mc * T + tt * TT:
                                    mc * T + tt * TT + TT],
                                acc[:, mc * T + tt * TT:
                                    mc * T + tt * TT + TT],
                                pb[:], ALU.add),
                            waits=[mm], inc="vs")
                wo_busy[id(pb)] = ad
                last_acc_add = ad
        wslice_free["o"] = ("ts", b.cnt["ts"])
    attn_done = b.snap()
    if debug:
        b.dma("gpsimd", dbg["dbg_acc_p2"][:, :], acc[:], waits=attn_done)
        b.dma("gpsimd", dbg["dbg_qb"][:, :], qb[:], waits=attn_done)
        b.dma("gpsimd", dbg["dbg_kb"][:, :], kb[:], waits=attn_done)
        b.dma("gpsimd", dbg["dbg_vb"][:, :], vb[:], waits=attn_done)
        b.dma("gpsimd", dbg["dbg_oT"][:, :], oT[:], waits=attn_done)

    trmA = b.emit("tensor", lambda e: e.transpose(
        P[4][0:16, 0:128], stat[:, 16:32], identf[:]),
        waits=attn_done + [ps_busy.get(id(P[4])), wo_busy.get(id(P[4]))],
        inc="ts")
    trmB = b.emit("tensor", lambda e: e.transpose(
        P[5][0:16, 0:128], stat[:, 32:48], identf[:]),
        waits=[ps_busy.get(id(P[5])), wo_busy.get(id(P[5]))], inc="ts")
    tevA = b.emit("vector", lambda e: e.tensor_copy(
        trscb[0:16, 0:128], P[4][0:16, 0:128]), waits=[trmA], inc="vs")
    tevB = b.emit("vector", lambda e: e.tensor_copy(
        trscb[0:16, 128:256], P[5][0:16, 0:128]), waits=[trmB], inc="vs")
    ps_busy[id(P[4])] = tevA
    ps_busy[id(P[5])] = tevB
    bc_last = None
    for ti in range(NTI):
        pb = P[4 + (ti % 2)]
        mmM = b.emit("tensor", lambda e, ti=ti, pb=pb: e.matmul(
            pb[:, 0:128], sel16[0:16, ti * 128:(ti + 1) * 128],
            trscb[0:16, 0:128], start=True, stop=True),
            waits=[tevA, tevB, consts_done, ps_busy.get(id(pb)),
                   wo_busy.get(id(pb))], inc="ts")
        mmR = b.emit("tensor", lambda e, ti=ti, pb=pb: e.matmul(
            pb[:, 128:256], sel16[0:16, ti * 128:(ti + 1) * 128],
            trscb[0:16, 128:256], start=True, stop=True), inc="ts")
        evM = b.emit("vector", lambda e, ti=ti, pb=pb: e.tensor_copy(
            mu2bc[:, ti * 128:(ti + 1) * 128], pb[:, 0:128]),
            waits=[mmM], inc="vs")
        evR = b.emit("vector", lambda e, ti=ti, pb=pb: e.tensor_copy(
            r2bc[:, ti * 128:(ti + 1) * 128], pb[:, 128:256]),
            waits=[mmR], inc="vs")
        ps_busy[id(pb)] = evR
        bc_last = evR
    xh_done = []
    tprev = None
    for kc in range(NKC):
        t1 = b.emit("vector", lambda e, kc=kc: e.tensor_tensor(
            xh_tmp[:], xnT[:, kc * T:(kc + 1) * T], mu2bc[:], ALU.subtract),
            waits=[bc_last], hard=[tprev], inc="vs")
        t2 = b.emit("vector", lambda e, kc=kc: e.tensor_tensor(
            xhatT[:, kc * T:(kc + 1) * T], xh_tmp[:], r2bc[:], ALU.mult),
            hard=[t1], inc="vs")
        tprev = t2
        xh_done.append(t2)
    ccast = b.dma("gpsimd", centb[:], dcent[:, :], waits=attn_done,
                  chain="cs")
    trc = None
    for kc in range(NKC):
        trc = b.emit("tensor", lambda e, kc=kc: e.transpose(
            PT0[:, kc * 8:(kc + 1) * 8], centb[:, kc * 128:(kc + 1) * 128],
            identb[0:8, 0:8]),
            waits=[ccast, pt_busy.get(id(PT0))], inc="ts")
    cev = b.emit("vector", lambda e: e.tensor_copy(centT[:], PT0[:, 0:64]),
                 waits=[trc], inc="vs")
    pt_busy[id(PT0)] = cev
    aff_last = None
    for tt in range(NTT):
        pb = P[4 + (tt % 2)]
        mm = None
        for kc in range(NKC):
            mm = b.emit("tensor", lambda e, kc=kc, tt=tt, pb=pb: e.matmul(
                pb[0:8, :], centT[:, kc * 8:(kc + 1) * 8],
                xnT[:, kc * T + tt * TT: kc * T + tt * TT + TT],
                start=(kc == 0), stop=(kc == NKC - 1)),
                waits=[cev, ps_busy.get(id(pb))], inc="ts")
        ev = b.emit("vector", lambda e, tt=tt, pb=pb: e.tensor_copy(
            affem[:, tt * TT:(tt + 1) * TT], pb[0:8, :]), waits=[mm],
            inc="vs")
        ps_busy[id(pb)] = ev
        aff_last = ev
    atrd = None
    for g in range(2):
        pb = P[4 + g]
        tr = None
        for j in range(8):
            ti = g * 8 + j
            tr = b.emit("tensor", lambda e, ti=ti, j=j, pb=pb: e.transpose(
                pb[:, j * 8:(j + 1) * 8], affem[:, ti * 128:(ti + 1) * 128],
                identf[0:8, 0:8]),
                waits=[aff_last, ps_busy.get(id(pb))], inc="ts")
        ev = b.emit("vector", lambda e, g=g, pb=pb: e.tensor_copy(
            afftm[:, g * 64:(g + 1) * 64], pb[:, 0:64]), waits=[tr], inc="vs")
        ps_busy[id(pb)] = ev
        atrd = ev
    pvc = [None]

    def vch2(fn, waits=()):
        pvc[0] = b.emit("vector", fn, waits=waits, hard=[pvc[0]], inc="vs")
        return pvc[0]

    mc_ = vch2(lambda e: e.tensor_reduce(
        stat[:, 48:64], afftm[:].rearrange("p (ti e) -> p ti e", e=8),
        AX.X, ALU.max), waits=[atrd])
    for ti in range(NTI):
        vch2(lambda e, ti=ti: e.tensor_scalar(
            ismax[:, ti * 8:(ti + 1) * 8], afftm[:, ti * 8:(ti + 1) * 8],
            stat[:, 48 + ti:49 + ti], None, ALU.is_ge))
    for ti in range(NTI):
        vch2(lambda e, ti=ti: e.tensor_tensor(
            ismax[:, ti * 8:(ti + 1) * 8], ismax[:, ti * 8:(ti + 1) * 8],
            c916[:], ALU.mult), waits=[consts_done])
    vch2(lambda e: e.tensor_scalar(
        ismax[:], ismax[:], -1.0, 9.0, ALU.mult, ALU.add))
    em_ = vch2(lambda e: e.tensor_reduce(
        stat[:, 64:80], ismax[:].rearrange("p (ti e) -> p ti e", e=8),
        AX.X, ALU.min))
    msk_last = None
    for ti in range(NTI):
        msk_last = vch2(lambda e, ti=ti: e.tensor_scalar(
            ismax[:, ti * 8:(ti + 1) * 8], iota8[:],
            stat[:, 64 + ti:65 + ti], None, ALU.is_equal))
    alp = b.emit("scalar", lambda e: e.activation(
        affem[:], affem[:], AF.Sigmoid),
        waits=[atrd], inc="ss")
    mtr = None
    for g in range(4):
        pb = P[4 + (g % 2)]
        tr = None
        for j in range(4):
            ti = g * 4 + j
            tr = b.emit("tensor", lambda e, ti=ti, j=j, pb=pb: e.transpose(
                pb[0:8, j * 128:(j + 1) * 128],
                ismax[:, ti * 8:(ti + 1) * 8], identf[:]),
                waits=[msk_last, ps_busy.get(id(pb))], inc="ts")
        ev = b.emit("vector", lambda e, g=g, pb=pb: e.tensor_copy(
            maskem[:, g * 512:(g + 1) * 512], pb[0:8, 0:512]),
            waits=[tr], inc="vs")
        ps_busy[id(pb)] = ev
        mtr = ev
    wem_done = b.emit("vector", lambda e: e.tensor_tensor(
        w_em[:], maskem[:], affem[:], ALU.mult), waits=[alp],
        hard=[mtr], inc="vs")
    selc = b.dma("sync", selb[:], dc["c_sel"][:, :], waits=attn_done,
                 chain="sel")
    p25_done = b.snap()
    if debug:
        b.dma("gpsimd", dbg["dbg_xhatT"][:, :], xhatT[:], waits=p25_done)
        b.dma("gpsimd", dbg["dbg_wem"][:, :], w_em[:], waits=p25_done)
        b.dma("gpsimd", dbg["dbg_maskem"][:, :], maskem[:], waits=p25_done)
        b.dma("gpsimd", dbg["dbg_alpha"][:, :], affem[:], waits=p25_done)

    w1r = dw1.rearrange("e (kc p) c -> e kc p c", p=128)
    w2r = dw2.rearrange("e (fc p) c -> e fc p c", p=128)
    TPS = [P[0], P[1]]
    ZPS = [P[2], P[3]]
    tps_busy = {}
    zps_busy = {}
    w1qs, w2qs = [w1q0, w1q1], [w2q0, w2q1]
    hbufs = [hbuf0, hbuf1]
    b1ts, gamTs, betTs = [b1t0, b1t1], [gamT0, gamT1], [betT0, betT1]
    pb1s, pgms, pbts = [pb1a, pb1b], [pgma, pgmb], [pbta, pbtb]
    QF = 8
    NSTEP = NE * 4
    NGT = NSTEP * 4

    def stream_w1_q(e_, q, buf, extra_waits):
        return b.dma("gpsimd",
                     w1qs[buf][:].rearrange("p (kc c) -> p kc c", kc=NKC),
                     w1r[e_, :, :, q * 1024:(q + 1) * 1024]
                     .rearrange("kc p c -> p kc c"),
                     waits=extra_waits, chain="w1")

    def stream_w2_q(e_, q, buf, extra_waits):
        return b.dma("gpsimd",
                     w2qs[buf][:].rearrange("p (fc c) -> p fc c", fc=QF),
                     w2r[e_, q * QF:(q + 1) * QF]
                     .rearrange("fc p c -> p fc c"),
                     waits=extra_waits, chain="w2")

    pstate = {}

    def param_dmas(e_, extra_waits):
        pi = e_ % 2
        prev = pstate.get(e_ - 2, {})
        st = {}
        st["l1"] = b.dma("gpsimd", pb1s[pi][:],
                         db1.rearrange("e (a c) -> e a c", c=128)[e_],
                         waits=list(extra_waits) + [prev.get("tb1")],
                         chain="pp")
        st["lg"] = b.dma("gpsimd", pgms[pi][:],
                         dgam.rearrange("e (a c) -> e a c", c=128)[e_],
                         waits=[prev.get("tgb")], chain="pp")
        st["lb"] = b.dma("gpsimd", pbts[pi][:],
                         dbet.rearrange("e (a c) -> e a c", c=128)[e_],
                         waits=[prev.get("tgb")], chain="pp")
        pstate[e_] = st

    def param_compute(e_, extra_waits):
        pi = e_ % 2
        st = pstate[e_]
        prev = pstate.get(e_ - 2, {})
        t1 = b.emit("tensor", lambda e, pi=pi: e.transpose(
            P[4][:, 0:32], pb1s[pi][:], identf[0:32, 0:32]),
            waits=list(extra_waits) + [st["l1"], ps_busy.get(id(P[4]))],
            inc="ts")
        st["tb1"] = ("ts", b.cnt["ts"])
        e1 = b.emit("vector", lambda e, pi=pi: e.tensor_copy(
            b1ts[pi][:], P[4][:, 0:32]),
            waits=[t1, prev.get("rl_last")], inc="vs")
        ps_busy[id(P[4])] = e1
        st["b1"] = e1
        t3 = b.emit("tensor", lambda e, pi=pi: e.transpose(
            P[5][:, 0:8], pgms[pi][:], identf[0:8, 0:8]),
            waits=[st["lg"], ps_busy.get(id(P[5]))], inc="ts")
        t4 = b.emit("tensor", lambda e, pi=pi: e.transpose(
            P[5][:, 8:16], pbts[pi][:], identf[0:8, 0:8]),
            waits=[st["lb"]], inc="ts")
        st["tgb"] = ("ts", b.cnt["ts"])
        b.emit("vector", lambda e, pi=pi: e.tensor_copy(
            gamTs[pi][:], P[5][:, 0:8]), waits=[t4], inc="vs")
        e4 = b.emit("vector", lambda e, pi=pi: e.tensor_copy(
            betTs[pi][:], P[5][:, 8:16]), inc="vs")
        ps_busy[id(P[5])] = e4
        st["gb"] = e4

    def emit_wbc(e_):
        wbc_done = None
        for tt in range(NTT):
            pb = P[4] if (tt % 2 == 0) else P[5]
            mm = b.emit("tensor", lambda e, e_=e_, tt=tt, pb=pb: e.matmul(
                pb[:], selb[:, e_ * 128:(e_ + 1) * 128],
                w_em[:, tt * TT:(tt + 1) * TT], start=True, stop=True),
                waits=[selc, wem_done, ps_busy.get(id(pb)),
                       zps_busy.get(id(pb))], inc="ts")
            ev = b.emit("vector", lambda e, tt=tt, pb=pb: e.tensor_copy(
                wbcb[:, tt * TT:(tt + 1) * TT], pb[:]),
                waits=[mm], inc="vs")
            ps_busy[id(pb)] = ev
            wbc_done = ev
        pstate[e_]["wbc"] = wbc_done

    zlast = [None]
    hbuf_busy = [None, None]
    tbuf_busy = [None]
    w1q_busy = [None, None]
    w2q_busy = [None, None]
    hd_done = {}

    def emit_hd(gti, extra=()):
        e_ = gti // 16
        tt = gti % 4
        pi2 = e_ % 2
        hb = hbufs[gti % 2]
        last = None
        for kc in range(NKC):
            last = b.emit(
                "vector",
                lambda e, kc=kc, tt=tt, pi2=pi2, hb=hb: e.tensor_scalar(
                    hb[:, kc * TT:(kc + 1) * TT],
                    xhatT[:, kc * T + tt * TT: kc * T + tt * TT + TT],
                    gamTs[pi2][:, kc:kc + 1], betTs[pi2][:, kc:kc + 1],
                    ALU.mult, ALU.add),
                waits=list(extra) + [xh_done[kc], hbuf_busy[gti % 2]],
                inc="vs")
        hd_done[gti] = last

    w1_ready = [None, None]
    w2_ready = [None, None]
    w1_ready[0] = stream_w1_q(0, 0, 0, p25_done)
    w2_ready[0] = stream_w2_q(0, 0, 0, p25_done)
    w1_ready[1] = stream_w1_q(0, 1, 1, p25_done)
    w2_ready[1] = stream_w2_q(0, 1, 1, p25_done)
    param_dmas(0, p25_done)
    param_compute(0, p25_done)
    emit_wbc(0)
    param_dmas(1, [])
    b2ld = b.dma("gpsimd", b2all[:], db2[:, :], waits=p25_done, chain="pp")
    emit_hd(0, p25_done)

    for s in range(NSTEP):
        e_, q = divmod(s, 4)
        buf = s % 2
        pi = e_ % 2
        est = pstate[e_]
        for tt in range(NTT):
            gti = s * 4 + tt
            hb = hbufs[gti % 2]
            tb = tbufq
            relu_ms = {}
            for fcl in range(QF):
                fc = q * QF + fcl
                bank = TPS[fcl % 2]
                t_mm_last = None
                for kc in range(NKC):
                    wts = [hd_done[gti], w1_ready[buf]]
                    if kc == 0:
                        wts.append(tps_busy.get(id(bank)))
                    t_mm_last = b.emit(
                        "tensor",
                        lambda e, kc=kc, fcl=fcl, bank=bank, buf=buf, hb=hb:
                        e.matmul(
                            bank[:],
                            w1qs[buf][:].rearrange("p (kc c) -> p kc c",
                                                   kc=NKC)
                            [:, kc, fcl * 128:(fcl + 1) * 128],
                            hb[:, kc * TT:(kc + 1) * TT],
                            start=(kc == 0), stop=(kc == NKC - 1)),
                        waits=wts, inc="ts")
                wrl = [t_mm_last, est["b1"], tbuf_busy[0]]
                if s == 0 and tt == 0 and fcl == 0:
                    wrl += p25_done
                rl = b.emit(
                    "scalar",
                    lambda e, fcl=fcl, fc=fc, bank=bank, pi=pi, tb=tb:
                    e.activation(
                        tb[:, fcl * TT:(fcl + 1) * TT], bank[:],
                        AF.Relu, bias=b1ts[pi][:, fc:fc + 1], scale=1.0),
                    waits=wrl, inc="ss")
                tps_busy[id(bank)] = rl
                relu_ms[fcl] = rl
            hbuf_busy[gti % 2] = ("ts", b.cnt["ts"])
            if gti + 1 < NGT:
                emit_hd(gti + 1)
            for mc in range(NKC):
                bank = ZPS[mc % 2]
                z_mm_last = None
                for fcl in range(QF):
                    wts = [relu_ms[fcl], w2_ready[buf]]
                    if fcl == 0:
                        wts.append(zps_busy.get(id(bank)))
                    z_mm_last = b.emit(
                        "tensor",
                        lambda e, fcl=fcl, mc=mc, bank=bank, buf=buf, tb=tb:
                        e.matmul(
                            bank[:],
                            w2qs[buf][:].rearrange("p (fc c) -> p fc c",
                                                   fc=QF)
                            [:, fcl, mc * 128:(mc + 1) * 128],
                            tb[:, fcl * TT:(fcl + 1) * TT],
                            start=(fcl == 0),
                            stop=(fcl == QF - 1),
                            skip_group_check=True),
                        waits=wts, inc="ts")
                zm = b.emit(
                    "vector",
                    lambda e, tt=tt, bank=bank: e.tensor_tensor(
                        ztmp[:], bank[:],
                        wbcb[:, tt * TT:(tt + 1) * TT], ALU.mult),
                    waits=[z_mm_last],
                    hard=[est["wbc"], zlast[0]], inc="vs")
                zps_busy[id(bank)] = zm
                zlast[0] = b.emit(
                    "vector",
                    lambda e, mc=mc, tt=tt: e.tensor_tensor(
                        acc[:, mc * T + tt * TT: mc * T + tt * TT + TT],
                        acc[:, mc * T + tt * TT: mc * T + tt * TT + TT],
                        ztmp[:], ALU.add),
                    waits=[last_acc_add], hard=[zm], inc="vs")
            tbuf_busy[0] = ("ts", b.cnt["ts"])
        w1q_busy[buf] = ("ts", b.cnt["ts"])
        w2q_busy[buf] = ("ts", b.cnt["ts"])
        if s + 2 < NSTEP:
            e2_, q2 = divmod(s + 2, 4)
            w1_ready[buf] = stream_w1_q(e2_, q2, buf, [w1q_busy[buf]])
            w2_ready[buf] = stream_w2_q(e2_, q2, buf, [w2q_busy[buf]])
        if q == 0 and 2 <= e_ + 1 < NE:
            param_dmas(e_ + 1, [])
        if q == 2 and e_ + 1 < NE:
            param_compute(e_ + 1, [])
        if q == 3:
            pstate[e_]["rl_last"] = ("ss", b.cnt["ss"])
            if e_ + 1 < NE:
                emit_wbc(e_ + 1)
    for tt in range(NTT):
        for mc in range(NKC):
            bank = ZPS[mc % 2]
            mm = b.emit(
                "tensor",
                lambda e, mc=mc, tt=tt, bank=bank: e.matmul(
                    bank[:], b2all[:, mc * 128:(mc + 1) * 128],
                    w_em[:, tt * TT:(tt + 1) * TT], start=True, stop=True),
                waits=[b2ld, wem_done, zps_busy.get(id(bank))], inc="ts")
            ad = b.emit(
                "vector",
                lambda e, mc=mc, tt=tt, bank=bank: e.tensor_tensor(
                    acc[:, mc * T + tt * TT: mc * T + tt * TT + TT],
                    acc[:, mc * T + tt * TT: mc * T + tt * TT + TT],
                    bank[:], ALU.add),
                waits=[mm], inc="vs")
            zps_busy[id(bank)] = ad
            zlast[0] = ad
    ffn_done = b.snap()

    outb = [out_tm0, out_tm1]
    ost_busy = [None, None]
    final_dma = None
    last_store = [None, None]
    for ti in range(NTI):
        ob = outb[ti % 2]
        evs = None
        for half in range(2):
            pb = P[4 + half]
            tr = None
            for j in range(4):
                kc = half * 4 + j
                tr = b.emit("tensor", lambda e, kc=kc, j=j, ti=ti, pb=pb:
                            e.transpose(
                                pb[:, j * 128:(j + 1) * 128],
                                acc[:, kc * T + ti * 128:
                                    kc * T + ti * 128 + 128], identf[:]),
                            waits=ffn_done + [ps_busy.get(id(pb))],
                            inc="ts")
            ev = b.emit("vector", lambda e, half=half, ob=ob, pb=pb:
                        e.tensor_copy(
                            ob[:, half * 512:(half + 1) * 512], pb[:]),
                        waits=[tr, ost_busy[ti % 2]], inc="vs")
            ps_busy[id(pb)] = ev
            evs = ev
        st = b.dma("gpsimd", dout.rearrange("(ti p) c -> ti p c", p=128)[ti],
                   ob[:], waits=[evs], chain="o%d" % (ti % 2))
        ost_busy[ti % 2] = st
        final_dma = st
        last_store[ti % 2] = st
    b.emit("gpsimd", lambda e: e.nop(),
           waits=[last_store[0], last_store[1]])

    with nc.Block() as block:
        semstack = b.finalize(block)
    semstack.close()
    ovstack.close()
    al.close()
    return nc


def kernel(L0_attn_x, L0_attn_Wq, L0_attn_Wk, L0_attn_Wv, L0_attn_Wo,
           L0_ffn_centroids, L0_ffn_w1, L0_ffn_b1, L0_ffn_w2, L0_ffn_b2,
           L0_ffn_gamma, L0_ffn_beta):
    x = np.ascontiguousarray(np.asarray(L0_attn_x, dtype=np.float32))
    assert x.shape == (8, T, D)
    shared = {
        "L0_attn_Wq": np.ascontiguousarray(L0_attn_Wq, dtype=np.float32),
        "L0_attn_Wk": np.ascontiguousarray(L0_attn_Wk, dtype=np.float32),
        "L0_attn_Wv": np.ascontiguousarray(L0_attn_Wv, dtype=np.float32),
        "L0_attn_Wo": np.ascontiguousarray(L0_attn_Wo, dtype=np.float32),
        "L0_ffn_centroids": np.ascontiguousarray(L0_ffn_centroids,
                                                 dtype=np.float32),
        "L0_ffn_w1": np.ascontiguousarray(L0_ffn_w1, dtype=np.float32),
        "L0_ffn_b1": np.ascontiguousarray(L0_ffn_b1, dtype=np.float32),
        "L0_ffn_w2": np.ascontiguousarray(L0_ffn_w2, dtype=np.float32),
        "L0_ffn_b2": np.ascontiguousarray(L0_ffn_b2, dtype=np.float32),
        "L0_ffn_gamma": np.ascontiguousarray(L0_ffn_gamma, dtype=np.float32),
        "L0_ffn_beta": np.ascontiguousarray(L0_ffn_beta, dtype=np.float32),
    }
    shared.update(make_consts())
    nc = build_program()
    in_maps = []
    for c in range(8):
        m = dict(shared)
        m["L0_attn_x"] = np.ascontiguousarray(x[c])
        in_maps.append(m)
    res = run_bass_kernel_spmd(nc, in_maps, core_ids=list(range(8)))
    return np.stack([np.asarray(r["out"], dtype=np.float32)
                     for r in res.results], axis=0)


# version 31
# speedup vs baseline: 1.2141x; 1.2141x over previous
import contextlib
import numpy as np
import ml_dtypes
import concourse.bass as bass
import concourse.mybir as mybir
from concourse.bass_utils import run_bass_kernel_spmd

F32 = mybir.dt.float32
BF16 = mybir.dt.bfloat16
AF = mybir.ActivationFunctionType
ALU = mybir.AluOpType
AX = mybir.AxisListType

ENGINES = ["sync", "scalar", "vector", "tensor", "gpsimd"]
COMPUTE_SEMS = {"ts": "tensor", "vs": "vector", "ss": "scalar", "gs": "gpsimd"}

D = 1024
T = 2048
NKC = 8
TT = 512
NTT = T // TT
NTI = 16
DK = 64
NE = 8
HALF_FC = 16
BIG = 1.0e30
EPS = 1.0e-5


class Builder:
    def __init__(self, nc):
        self.nc = nc
        self.ops = {e: [] for e in ENGINES}
        self.cnt = {}
        self.maxwait = {}
        self.sem_names = []

    def sem(self, name):
        if name not in self.cnt:
            self.cnt[name] = 0
            self.sem_names.append(name)
        return name

    def emit(self, eng, fn, waits=(), inc=None, inc_amt=1, hard=()):
        w = []
        for (soft, wl) in ((True, waits), (False, hard)):
            for x in wl:
                if x is None:
                    continue
                (s, v) = x
                if v <= 0:
                    continue
                key = (eng, s)
                if self.maxwait.get(key, -1) >= v:
                    continue
                self.maxwait[key] = v
                w.append((s, v))
        post = None
        if inc is not None:
            self.sem(inc)
            self.cnt[inc] += inc_amt
            post = (inc, self.cnt[inc])
        self.ops[eng].append((fn, w, inc, inc_amt))
        return post

    def dma(self, queue, out, in_, waits=(), chain=None):
        sem = "q_" + (chain if chain is not None else queue)
        self.sem(sem)
        prev = (sem, self.cnt[sem])
        return self.emit(queue, lambda e, o=out, i=in_: e.dma_start(o, i),
                         waits=list(waits) + [prev], inc=sem, inc_amt=16)

    def snap(self):
        return [(s, self.cnt.get(s, 0)) for s in ("ts", "vs", "ss", "gs")]

    def replay(self, eng_name, eng, sems):
        for (fn, waits, inc, amt) in self.ops[eng_name]:
            for (s, v) in waits:
                eng.wait_ge(sems[s], v)
            inst = fn(eng)
            if inc is not None:
                inst.then_inc(sems[inc], amt)

    def finalize(self, block):
        stack = contextlib.ExitStack()
        sems = {}
        for name in self.sem_names:
            sems[name] = stack.enter_context(self.nc.semaphore(name))
        for e in ENGINES:
            getattr(block, e)(lambda eng, n=e: self.replay(n, eng, sems))
        return stack


class Overlay:
    def __init__(self, nc, total_bytes):
        self.nc = nc
        self.total = total_bytes
        self.views = []

    def view(self, name, shape, dtype, off):
        per_part = int(np.prod(shape[1:])) * mybir.dt.size(dtype)
        assert off % 4 == 0 and off + per_part <= self.total, (
            name, off, per_part, self.total)
        h = self.nc.virtual_sbuf_tensor(name, shape, dtype)
        self.views.append((h, off))
        return h

    def allocate(self, stack, arena_name):
        arena = stack.enter_context(
            self.nc.sbuf_tensor(arena_name, [128, self.total], mybir.dt.uint8))
        base = self.nc.lookup_mloc(arena).addr
        for (h, off) in self.views:
            self.nc._tensor(h.name, h.shape, h.dtype, type="SB")
            mloc = self.nc.lookup_mloc(h)
            mloc.allocated = True
            mloc.addr = base + off


def make_consts():
    bf = ml_dtypes.bfloat16
    inv = 10000.0 ** (-np.arange(0, DK, 2) / DK)
    ang = np.outer(np.arange(T), inv)
    cos = np.repeat(np.cos(ang), 2, axis=1)
    sin = np.repeat(np.sin(ang), 2, axis=1)
    cos_fm = np.tile(cos.T, (2, 1)).astype(np.float32)
    sin_fm = np.tile(sin.T, (2, 1)).astype(np.float32)
    p = np.zeros((128, 128), np.float32)
    for j in range(64):
        p[2 * j + 1, 2 * j] = -1.0
        p[2 * j, 2 * j + 1] = 1.0
    ident = np.eye(128, dtype=np.float32)
    mask = np.zeros((128, 1024), np.float32)
    pidx = np.arange(128)
    bigp = 0.125 * BIG
    for c in range(384, 512):
        mask[:, c] = np.where((c - 384) <= pidx, 0.0, bigp)
    mask[:, 512:] = bigp
    iota8 = np.tile(np.arange(8, dtype=np.float32), (128, 1))
    c916 = np.tile(9.0 - np.arange(8, dtype=np.float32), (128, 1))
    sel = np.zeros((8, 8 * 128), np.float32)
    for e in range(8):
        sel[e, e * 128:(e + 1) * 128] = 1.0
    sel16 = np.zeros((16, 16 * 128), np.float32)
    for ti in range(16):
        sel16[ti, ti * 128:(ti + 1) * 128] = 1.0
    return {
        "c_cos": cos_fm.astype(bf), "c_sin": sin_fm.astype(bf),
        "c_p128": p.astype(bf), "c_ident": ident.astype(bf),
        "c_identf": ident, "c_mask": mask.astype(bf),
        "c_iota8": iota8, "c_c916": c916, "c_sel": sel.astype(bf),
        "c_sel16": sel16.astype(bf),
    }


def build_program(debug=False):
    nc = bass.Bass("TRN2", target_bir_lowering=False)
    b = Builder(nc)

    dx = nc.dram_tensor("L0_attn_x", [T, D], F32, kind="ExternalInput")
    dwq = nc.dram_tensor("L0_attn_Wq", [D, D], F32, kind="ExternalInput")
    dwk = nc.dram_tensor("L0_attn_Wk", [D, D], F32, kind="ExternalInput")
    dwv = nc.dram_tensor("L0_attn_Wv", [D, D], F32, kind="ExternalInput")
    dwo = nc.dram_tensor("L0_attn_Wo", [D, D], F32, kind="ExternalInput")
    dcent = nc.dram_tensor("L0_ffn_centroids", [NE, D], F32,
                           kind="ExternalInput")
    dw1 = nc.dram_tensor("L0_ffn_w1", [NE, D, 4096], F32,
                         kind="ExternalInput")
    db1 = nc.dram_tensor("L0_ffn_b1", [NE, 4096], F32, kind="ExternalInput")
    dw2 = nc.dram_tensor("L0_ffn_w2", [NE, 4096, D], F32,
                         kind="ExternalInput")
    db2 = nc.dram_tensor("L0_ffn_b2", [NE, D], F32, kind="ExternalInput")
    dgam = nc.dram_tensor("L0_ffn_gamma", [NE, D], F32, kind="ExternalInput")
    dbet = nc.dram_tensor("L0_ffn_beta", [NE, D], F32, kind="ExternalInput")
    dout = nc.dram_tensor("out", [T, D], F32, kind="ExternalOutput")
    drows = nc.dram_tensor("rows_scratch", [2, T], F32)
    dbg = {}
    if debug:
        for (n, shp, dt) in [("dbg_acc_p1", [128, NKC * T], F32),
                             ("dbg_xnT", [128, NKC * T], BF16),
                             ("dbg_stat", [128, 84], F32),
                             ("dbg_acc_p2", [128, NKC * T], F32),
                             ("dbg_xhatT", [128, NKC * T], BF16),
                             ("dbg_wem", [8, T], BF16),
                             ("dbg_maskem", [8, T], F32),
                             ("dbg_alpha", [8, T], F32),
                             ("dbg_qb", [128, T], BF16),
                             ("dbg_w1h", [128, NKC * HALF_FC * 128], BF16),
                             ("dbg_w2h", [128, HALF_FC * D], BF16),
                             ("dbg_tbuf", [128, HALF_FC * TT], BF16),
                             ("dbg_hbuf", [128, NKC * TT], BF16),
                             ("dbg_wbcb", [128, T], BF16),
                             ("dbg_b1t", [128, 32], F32),
                             ("dbg_gamT", [128, 8], F32),
                             ("dbg_betT", [128, 8], F32),
                             ("dbg_kb", [128, T], BF16),
                             ("dbg_vb", [128, T], BF16),
                             ("dbg_oT", [128, T], BF16)]:
            dbg[n] = nc.dram_tensor(n, shp, dt, kind="ExternalOutput")
    dc = {}
    for (n, shp, dt) in [("c_cos", [128, T], BF16), ("c_sin", [128, T], BF16),
                         ("c_p128", [128, 128], BF16),
                         ("c_ident", [128, 128], BF16),
                         ("c_identf", [128, 128], F32),
                         ("c_mask", [128, 1024], BF16),
                         ("c_iota8", [128, 8], F32),
                         ("c_c916", [128, 8], F32),
                         ("c_sel", [8, 8 * 128], BF16),
                         ("c_sel16", [16, 16 * 128], BF16)]:
        dc[n] = nc.dram_tensor(n, shp, dt, kind="ExternalInput")

    al = contextlib.ExitStack()

    acc = al.enter_context(nc.sbuf_tensor("acc", [128, NKC * T], F32))
    xhatT = al.enter_context(nc.sbuf_tensor("xhatT", [128, NKC * T], BF16))
    OVC = Overlay(nc, 1792)
    identb = OVC.view("identb", [128, 128], BF16, 0)
    identf = OVC.view("identf", [128, 128], F32, 512)
    iota8 = OVC.view("iota8", [128, 8], F32, 1024)
    c916 = OVC.view("c916", [128, 8], F32, 1056)
    onescol = OVC.view("onescol", [1, 128], BF16, 1088)
    stat = OVC.view("stat", [128, 84], F32, 1344)

    OVB = Overlay(nc, 32768)
    xnT = OVB.view("xnT", [128, NKC * T], BF16, 0)
    w1q0 = OVB.view("w1q0", [128, NKC * 8 * 128], BF16, 0)
    w1q1 = OVB.view("w1q1", [128, NKC * 8 * 128], BF16, 16384)

    OV = Overlay(nc, 78208)
    xstage = OV.view("xstage", [128, 2 * 1024], F32, 0)
    xnf = OV.view("xnf", [128, 1024], F32, 8192)
    xnb = OV.view("xnb", [128, 1024], BF16, 12288)
    xnf2 = OV.view("xnf2", [128, 1024], F32, 38912)
    xnb2 = OV.view("xnb2", [128, 1024], BF16, 43008)
    qb = OV.view("qb", [128, T], BF16, 0)
    kb = OV.view("kb", [128, T], BF16, 4096)
    vb = OV.view("vb", [128, T], BF16, 8192)
    qtmp = OV.view("qtmp", [128, 2 * TT], BF16, 12288)
    wqs = OV.view("wqs", [128, NKC * 128], BF16, 14336)
    wks = OV.view("wks", [128, NKC * 128], BF16, 16384)
    wvs = OV.view("wvs", [128, NKC * 128], BF16, 18432)
    wos = OV.view("wos", [128, 1024], BF16, 20480)
    S_p0 = OV.view("S_p0", [128, 2 * T], BF16, 22528)
    S_p1 = OV.view("S_p1", [128, 2 * T], BF16, 30720)
    A_sb = OV.view("A_sb", [128, 2 * T], BF16, 38912)
    AT_sb = OV.view("AT_sb", [128, T], BF16, 47104)
    oT = OV.view("oT", [128, T], BF16, 51200)
    cosb = OV.view("cosb", [128, T], BF16, 55296)
    sinb = OV.view("sinb", [128, T], BF16, 59392)
    maskm = OV.view("maskm", [128, 1024], BF16, 63488)
    p128 = OV.view("p128b", [128, 128], BF16, 65536)
    affem = OV.view("affem", [8, T], F32, 22528)
    afftm = OV.view("afftm", [128, 128], F32, 30720)
    ismax = OV.view("ismax", [128, 128], F32, 31232)
    maskem = OV.view("maskem", [8, T], F32, 31744)
    mu2bc = OV.view("mu2bc", [128, T], F32, 39936)
    r2bc = OV.view("r2bc", [128, T], F32, 48128)
    xh_tmp = OV.view("xh_tmp", [128, T], BF16, 56320)
    cent32 = OV.view("cent32", [8, 1024], F32, 60416)
    centb = OV.view("centb", [8, 1024], BF16, 64512)
    centT = OV.view("centT", [128, 64], BF16, 66560)
    trsc = OV.view("trsc", [32, 128], F32, 66688)
    rowsb = OV.view("rowsb", [1, 4096], BF16, 30720)
    w2q0 = OV.view("w2q0", [128, 8 * D], BF16, 0)
    w2q1 = OV.view("w2q1", [128, 8 * D], BF16, 16384)
    tbufq = OV.view("tbufq", [128, 8 * TT], BF16, 32768)
    hbuf0 = OV.view("hbuf0", [128, NKC * TT], BF16, 40960)
    hbuf1 = OV.view("hbuf1", [128, NKC * TT], BF16, 49152)
    ztmp = OV.view("ztmp", [128, TT], F32, 57344)
    wbcb = OV.view("wbcb", [128, T], BF16, 59392)
    w_em = OV.view("w_em", [8, T], BF16, 63488)
    b2all = OV.view("b2all", [8, 1024], BF16, 67584)
    pb1a = OV.view("pb1a", [32, 128], F32, 71680)
    pb1b = OV.view("pb1b", [32, 128], F32, 72192)
    pgma = OV.view("pgma", [8, 128], F32, 72704)
    pgmb = OV.view("pgmb", [8, 128], F32, 73216)
    pbta = OV.view("pbta", [8, 128], F32, 73728)
    pbtb = OV.view("pbtb", [8, 128], F32, 74240)
    b1t0 = OV.view("b1t0", [128, 32], F32, 74752)
    b1t1 = OV.view("b1t1", [128, 32], F32, 74880)
    gamT0 = OV.view("gamT0", [128, 8], F32, 75008)
    gamT1 = OV.view("gamT1", [128, 8], F32, 75040)
    betT0 = OV.view("betT0", [128, 8], F32, 75072)
    betT1 = OV.view("betT1", [128, 8], F32, 75104)
    selb = OV.view("selb", [8, 8 * 128], BF16, 76160)
    ltrb = OV.view("ltrb", [16, 256], BF16, 12288)
    lbc = OV.view("lbc", [128, T], BF16, 65792)
    trscb = OV.view("trscb", [16, 256], BF16, 60416)
    sel16 = OV.view("sel16", [16, 16 * 128], BF16, 69888)
    out_tm0 = OV.view("out_tm0", [128, 1024], F32, 0)
    out_tm1 = OV.view("out_tm1", [128, 1024], F32, 4096)

    P = [al.enter_context(nc.psum_tensor(f"P{i}", [128, 512], F32))
         for i in range(6)]
    PT0 = al.enter_context(nc.psum_tensor("PT0", [128, 1024], BF16))
    PT1 = al.enter_context(nc.psum_tensor("PT1", [128, 1024], BF16))
    PTs = [PT0, PT1]

    ovstack = contextlib.ExitStack()
    OVC.allocate(ovstack, "arenaC")
    OVB.allocate(ovstack, "arenaB")
    OV.allocate(ovstack, "arenaP")

    c_loads = []
    for (dst, src) in [(identb[:], dc["c_ident"][:, :]),
                       (identf[:], dc["c_identf"][:, :]),
                       (p128[:], dc["c_p128"][:, :]),
                       (iota8[:], dc["c_iota8"][:, :]),
                       (c916[:], dc["c_c916"][:, :]),
                       (maskm[:], dc["c_mask"][:, :]),
                       (cosb[:], dc["c_cos"][:, :]),
                       (sinb[:], dc["c_sin"][:, :]),
                       (sel16[:], dc["c_sel16"][:, :])]:
        c_loads.append(b.dma("gpsimd", dst, src, chain="cg"))
    ocs = b.emit("vector", lambda e: e.memset(onescol[:], 1.0), inc="vs")
    consts_done = c_loads[-1]

    xs = [xstage[:, 0:1024], xstage[:, 1024:2048]]
    xnfs = [xnf, xnf2]
    xnbs = [xnb, xnb2]
    psA = [P[4], P[5]]
    ps_busy = {}
    pt_busy = {}
    par_done = [None, None]
    xnb_free = [None, None]
    ln_loads = [b.dma("gpsimd", xs[ti % 2],
                      dx.rearrange("(ti p) c -> ti p c", p=128)[ti],
                      waits=[consts_done], chain="x%d" % (ti % 2))
                for ti in range(2)]
    for ti in range(NTI):
        pi2 = ti % 2
        sb = 48 * pi2
        xbuf = xs[pi2]
        xnf_t = xnfs[pi2]
        xnb_t = xnbs[pi2]
        w0 = [ln_loads[ti], par_done[pi2]]
        pv = [None]

        def vch(fn, waits=()):
            pv[0] = b.emit("vector", fn, waits=waits, hard=[pv[0]], inc="vs")
            return pv[0]

        vch(lambda e, xbuf=xbuf, sb=sb: e.reduce_sum(
            stat[:, sb + 0:sb + 1], xbuf, AX.X), waits=w0)
        vch(lambda e, sb=sb: e.tensor_scalar(
            stat[:, sb + 1:sb + 2], stat[:, sb + 0:sb + 1], 1.0 / D, None,
            ALU.mult))
        sq = b.emit("scalar", lambda e, xbuf=xbuf, sb=sb, xnf_t=xnf_t:
                    e.activation(
                        xnf_t[:], xbuf, AF.Square,
                        accum_out=stat[:, sb + 2:sb + 3]),
                    waits=w0, inc="ss")
        vch(lambda e, sb=sb: e.tensor_scalar(
            stat[:, sb + 3:sb + 4], stat[:, sb + 2:sb + 3], 1.0 / D, None,
            ALU.mult), waits=[sq])
        vch(lambda e, sb=sb: e.tensor_tensor(
            stat[:, sb + 4:sb + 5], stat[:, sb + 1:sb + 2],
            stat[:, sb + 1:sb + 2], ALU.mult))
        vch(lambda e, sb=sb: e.tensor_tensor(
            stat[:, sb + 4:sb + 5], stat[:, sb + 3:sb + 4],
            stat[:, sb + 4:sb + 5], ALU.subtract))
        ve = vch(lambda e, sb=sb: e.tensor_scalar(
            stat[:, sb + 5:sb + 6], stat[:, sb + 4:sb + 5], EPS, None,
            ALU.add))
        sz = b.emit("scalar", lambda e, sb=sb: e.activation(
            stat[:, sb + 6:sb + 7], stat[:, sb + 5:sb + 6], AF.Sqrt),
            waits=[ve], inc="ss")
        vch(lambda e, sb=sb: e.reciprocal(
            stat[:, sb + 7:sb + 8], stat[:, sb + 6:sb + 7]), waits=[sz])
        xnfo = vch(lambda e, xbuf=xbuf, sb=sb, xnf_t=xnf_t: e.tensor_scalar(
            xnf_t[:], xbuf, stat[:, sb + 1:sb + 2], stat[:, sb + 7:sb + 8],
            ALU.subtract, ALU.mult), waits=[sq])
        vch(lambda e, sb=sb, xnf_t=xnf_t: e.reduce_sum(
            stat[:, sb + 8:sb + 9], xnf_t[:], AX.X))
        vch(lambda e, ti=ti, sb=sb: e.tensor_scalar(
            stat[:, 16 + ti:17 + ti], stat[:, sb + 8:sb + 9], 1.0 / D, None,
            ALU.mult))
        sq2 = b.emit("scalar", lambda e, sb=sb, xnf_t=xnf_t, xnb_t=xnb_t:
                     e.activation(
                         xnb_t[:], xnf_t[:], AF.Square,
                         accum_out=stat[:, sb + 9:sb + 10]),
                     waits=[xnfo, xnb_free[pi2]], inc="ss")
        vch(lambda e, sb=sb: e.tensor_scalar(
            stat[:, sb + 10:sb + 11], stat[:, sb + 9:sb + 10], 1.0 / D, None,
            ALU.mult), waits=[sq2])
        vch(lambda e, ti=ti, sb=sb: e.tensor_tensor(
            stat[:, sb + 11:sb + 12], stat[:, 16 + ti:17 + ti],
            stat[:, 16 + ti:17 + ti], ALU.mult))
        vch(lambda e, sb=sb: e.tensor_tensor(
            stat[:, sb + 11:sb + 12], stat[:, sb + 10:sb + 11],
            stat[:, sb + 11:sb + 12], ALU.subtract))
        v2 = vch(lambda e, sb=sb: e.tensor_scalar(
            stat[:, sb + 12:sb + 13], stat[:, sb + 11:sb + 12], EPS, None,
            ALU.add))
        sz2 = b.emit("scalar", lambda e, sb=sb: e.activation(
            stat[:, sb + 13:sb + 14], stat[:, sb + 12:sb + 13], AF.Sqrt),
            waits=[v2], inc="ss")
        vch(lambda e, ti=ti, sb=sb: e.reciprocal(
            stat[:, 32 + ti:33 + ti], stat[:, sb + 13:sb + 14]),
            waits=[sz2])
        xnbo = vch(lambda e, xnf_t=xnf_t, xnb_t=xnb_t: e.tensor_copy(
            xnb_t[:], xnf_t[:]), waits=[sq2])
        vch(lambda e, sb=sb: e.tensor_scalar(
            stat[:, sb + 14:sb + 15], stat[:, sb + 7:sb + 8], 1.0, None,
            ALU.add))
        vch(lambda e, sb=sb: e.tensor_tensor(
            stat[:, sb + 15:sb + 16], stat[:, sb + 1:sb + 2],
            stat[:, sb + 7:sb + 8], ALU.mult))
        uo = vch(lambda e, xbuf=xbuf, sb=sb: e.tensor_scalar(
            xbuf, xbuf, stat[:, sb + 14:sb + 15], stat[:, sb + 15:sb + 16],
            ALU.mult, ALU.subtract))
        for half in range(2):
            pb = psA[half]
            tr = None
            for j in range(4):
                kc = half * 4 + j
                tr = b.emit("tensor", lambda e, kc=kc, j=j, pb=pb, xbuf=xbuf:
                            e.transpose(pb[:, j * 128:(j + 1) * 128],
                                        xbuf[:, kc * 128:(kc + 1) * 128],
                                        identf[:]),
                            waits=[uo, consts_done, ps_busy.get(id(pb))],
                            inc="ts")
            ev = b.emit("vector", lambda e, half=half, ti=ti, pb=pb:
                        e.tensor_copy(
                            acc[:].rearrange("p (kc t) -> p kc t", kc=NKC)
                            [:, half * 4:(half + 1) * 4,
                             ti * 128:(ti + 1) * 128], pb[:]),
                        waits=[tr], inc="vs")
            ps_busy[id(pb)] = ev
        pt = PTs[ti % 2]
        tr = None
        for kc in range(NKC):
            tr = b.emit("tensor", lambda e, kc=kc, pt=pt, xnb_t=xnb_t:
                        e.transpose(
                            pt[:, kc * 128:(kc + 1) * 128],
                            xnb_t[:, kc * 128:(kc + 1) * 128], identb[:]),
                        waits=[xnbo, pt_busy.get(id(pt))], inc="ts")
        xnb_free[pi2] = ("ts", b.cnt["ts"])
        ev = b.emit("vector", lambda e, ti=ti, pt=pt: e.tensor_copy(
            xnT[:].rearrange("p (kc t) -> p kc t", kc=NKC)
            [:, :, ti * 128:(ti + 1) * 128], pt[:]),
            waits=[tr], inc="vs")
        pt_busy[id(pt)] = ev
        par_done[pi2] = pv[0]
        if ti + 2 < NTI:
            ln_loads.append(b.dma(
                "gpsimd", xs[pi2],
                dx.rearrange("(ti p) c -> ti p c", p=128)[ti + 2],
                waits=[("ts", b.cnt["ts"])], chain="x%d" % (ti % 2)))
    p1_done = b.snap()
    if debug:
        b.dma("gpsimd", dbg["dbg_acc_p1"][:, :], acc[:], waits=p1_done)
        b.dma("gpsimd", dbg["dbg_xnT"][:, :], xnT[:], waits=p1_done)
        b.dma("gpsimd", dbg["dbg_stat"][:, :], stat[:, :], waits=p1_done)

    sp_busy = {}
    avp_busy = None
    wo_busy = {}
    wslice_free = {"qkv": None, "o": None}
    wcast_prev = None
    last_acc_add = None
    epi_mark = [None]
    lsc_prev = [None]
    zps_busy_dummy = {}

    def stream_wslice(ch, first):
        nonlocal wcast_prev
        outs = {}
        for (nm, wsb, dwt, kind) in [("q", wqs, dwq, "col"),
                                     ("k", wks, dwk, "col"),
                                     ("v", wvs, dwv, "col"),
                                     ("o", wos, dwo, "row")]:
            waits = [wslice_free["o" if nm == "o" else "qkv"]]
            if first:
                waits.append(consts_done)
            if kind == "col":
                src_ = dwt.rearrange("(kc p) c -> p kc c", p=128)[
                    :, :, ch * 128:(ch + 1) * 128]
                dst = wsb[:].rearrange("p (kc c) -> p kc c", kc=NKC)
            else:
                src_ = dwt.rearrange("(c p) m -> p c m", p=128)[:, ch, :]
                dst = wsb[:, 0:1024]
            outs[nm] = b.dma("gpsimd", dst, src_, waits=waits,
                             chain="ws_" + nm)
        wcast_prev = outs
        return outs

    for ch in range(NKC):
        wst = stream_wslice(ch, ch == 0)
        projp = [P[3], P[4]]
        pe_busy2 = {}
        qk_last = None
        rope_prev = {}
        for (wsb, dstb, wkey) in [(wqs, qb, "q"), (wks, kb, "k")]:
            for tt in range(NTT):
                pb = projp[tt % 2]
                w = p1_done + [wst[wkey], pe_busy2.get(id(pb)),
                               sp_busy.get(id(pb)), wo_busy.get(id(pb)),
                               ps_busy.get(id(pb))]
                mm = None
                for kc in range(NKC):
                    mm = b.emit("tensor", lambda e, kc=kc, tt=tt, wsb=wsb,
                                pb=pb: e.matmul(
                                    pb[:], wsb[:, kc * 128:(kc + 1) * 128],
                                    xnT[:, kc * T + tt * TT:
                                        kc * T + tt * TT + TT],
                                    start=(kc == 0), stop=(kc == NKC - 1)),
                                waits=w, inc="ts")
                sl = (tt % 2) * TT
                ev = b.emit("vector", lambda e, pb=pb, sl=sl: e.tensor_copy(
                    qtmp[:, sl:sl + TT], pb[:]), waits=[mm, epi_mark[0]],
                    hard=[rope_prev.get(sl)], inc="vs")
                rot = b.emit("tensor", lambda e, sl=sl, pb=pb: e.matmul(
                    pb[:], p128[:], qtmp[:, sl:sl + TT], start=True,
                    stop=True), waits=[ev], inc="ts")
                qc = b.emit("vector", lambda e, sl=sl, tt=tt, dstb=dstb:
                            e.tensor_tensor(
                                dstb[:, tt * TT:(tt + 1) * TT],
                                qtmp[:, sl:sl + TT],
                                cosb[:, tt * TT:(tt + 1) * TT], ALU.mult),
                            hard=[ev], inc="vs")
                rs = b.emit("vector", lambda e, sl=sl, tt=tt, pb=pb:
                            e.tensor_tensor(
                                qtmp[:, sl:sl + TT], pb[:],
                                sinb[:, tt * TT:(tt + 1) * TT], ALU.mult),
                            waits=[rot], hard=[qc], inc="vs")
                fin = b.emit("vector", lambda e, sl=sl, tt=tt, dstb=dstb:
                             e.tensor_tensor(
                                 dstb[:, tt * TT:(tt + 1) * TT],
                                 dstb[:, tt * TT:(tt + 1) * TT],
                                 qtmp[:, sl:sl + TT], ALU.add),
                             hard=[rs, qc], inc="vs")
                pe_busy2[id(pb)] = rs
                rope_prev[sl] = fin
                qk_last = fin
        v_done = None
        for tig in range(4):
            pb = projp[tig % 2]
            w = p1_done + [wst["v"], pe_busy2.get(id(pb))]
            mm = None
            for tl in range(4):
                ti = tig * 4 + tl
                for kc in range(NKC):
                    mm = b.emit("tensor", lambda e, kc=kc, ti=ti, tl=tl,
                                pb=pb: e.matmul(
                                    pb[:, tl * 128:(tl + 1) * 128],
                                    xnT[:, kc * T + ti * 128:
                                        kc * T + ti * 128 + 128],
                                    wvs[:, kc * 128:(kc + 1) * 128],
                                    start=(kc == 0), stop=(kc == NKC - 1)),
                                waits=w, inc="ts")
            ev = b.emit("vector", lambda e, tig=tig, pb=pb: e.tensor_copy(
                vb[:, tig * 512:(tig + 1) * 512], pb[:]), waits=[mm],
                inc="vs")
            pe_busy2[id(pb)] = ev
            v_done = ev
        wslice_free["qkv"] = ("ts", b.cnt["ts"])
        nm_prev = {}
        transp_end = {0: None, 1: None}
        prev_i = None

        def emit_transp_av(pi):
            nonlocal avp_busy, mmv_last
            mmv = None
            for h in range(2):
                hoff = h * T
                nb = pi + 1
                aev_g = {}
                tr = None
                for b0 in range(0, nb, 8):
                    bn = min(8, nb - b0)
                    pt = PTs[(b0 // 8) % 2]
                    tr = None
                    Spi = S_p0 if (pi % 2 == 0) else S_p1
                    for bb in range(bn):
                        blk = b0 + bb
                        tr = b.emit(
                            "tensor",
                            lambda e, hoff=hoff, blk=blk, bb=bb, pt=pt,
                            Spi=Spi: e.transpose(
                                pt[:, bb * 128:(bb + 1) * 128],
                                Spi[:, hoff + blk * 128:
                                    hoff + blk * 128 + 128], identb[:]),
                            waits=[nm_prev[h], pt_busy.get(id(pt))],
                            inc="ts")
                    aev = b.emit("vector",
                                 lambda e, b0=b0, bn=bn, pt=pt: e.tensor_copy(
                                     AT_sb[:, b0 * 128:(b0 + bn) * 128],
                                     pt[:, 0:bn * 128]),
                                 waits=[tr], inc="vs")
                    pt_busy[id(pt)] = aev
                    aev_g[b0 // 8] = aev
                transp_end[h] = ("ts", b.cnt["ts"])
                for blk in range(pi + 1):
                    wts = [aev_g[blk // 8], v_done]
                    if blk == 0 and h == 0:
                        wts.append(avp_busy)
                    mmv = b.emit(
                        "tensor",
                        lambda e, h=h, blk=blk, pi=pi: e.matmul(
                            P[5][h * 64:(h + 1) * 64, 0:128],
                            vb[:, blk * 128 + h * 64:
                               blk * 128 + h * 64 + 64],
                            AT_sb[:, blk * 128:(blk + 1) * 128],
                            start=(blk == 0), stop=(blk == pi),
                            tile_position=(0, h * 64)),
                        waits=wts, inc="ts")
            oev = b.emit("scalar", lambda e, pi=pi: e.copy(
                oT[:, pi * 128:(pi + 1) * 128], P[5][:, 0:128]),
                waits=[mmv, lsc_prev[0]], inc="ss")
            avp_busy = oev

        mmv_last = None

        def emit_smm_evacs(i, ex_mark):
            W = (i + 1) * 128
            nspan = (i // 4) + 1
            Ssb = S_p0 if (i % 2 == 0) else S_p1
            bankA = P[i % 2]
            bankB = P[2 + (i % 2)]
            for si in range(nspan):
                s0 = si * 512
                wdt = min(512, W - s0)
                mmAB = {}
                for (h, base, pb) in [(0, 0, bankA), (1, 64, bankB)]:
                    wts = [qk_last, sp_busy.get(id(pb)),
                           pe_busy2.get(id(pb)), wo_busy.get(id(pb))]
                    mmAB[h] = b.emit(
                        "tensor",
                        lambda e, i=i, s0=s0, wdt=wdt, base=base, pb=pb:
                        e.matmul(pb[:, 0:wdt],
                                 qb[base:base + 64, i * 128:(i + 1) * 128],
                                 kb[base:base + 64, s0:s0 + wdt],
                                 start=True, stop=True),
                        waits=wts, inc="ts")
                for (h, pb) in [(0, bankA), (1, bankB)]:
                    hoff = h * T
                    if si == nspan - 1:
                        r = i % 4
                        off = 384 - r * 128
                        ev = b.emit(
                            "vector",
                            lambda e, hoff=hoff, s0=s0, wdt=wdt, off=off,
                            pb=pb, Ssb=Ssb: e.scalar_tensor_tensor(
                                Ssb[:, hoff + s0: hoff + s0 + wdt],
                                pb[:, 0:wdt], -0.125,
                                maskm[:, off:off + wdt],
                                ALU.mult, ALU.add),
                            waits=[mmAB[h]] + list(ex_mark or []), inc="vs")
                    else:
                        ev = b.emit(
                            "scalar",
                            lambda e, hoff=hoff, s0=s0, wdt=wdt, pb=pb,
                            Ssb=Ssb: e.activation(
                                Ssb[:, hoff + s0: hoff + s0 + wdt],
                                pb[:, 0:wdt], AF.Copy, scale=-0.125),
                            waits=[mmAB[h]] + list(ex_mark or []), inc="ss")
                    sp_busy[id(pb)] = ev
            sev_marks[i] = ("ss", b.cnt["ss"])

        sev_marks = {}
        emit_smm_evacs(0, None)
        tr_m_prev = None
        for i in range(NTI):
            W = (i + 1) * 128
            if prev_i is not None:
                emit_transp_av(prev_i)
            Ssb_i = S_p0 if (i % 2 == 0) else S_p1
            ex = None
            for h in range(2):
                hoff = h * T
                lslot = (0 if h == 0 else 48) + i
                ex = b.emit(
                    "scalar", lambda e, hoff=hoff, W=W, h=h, lslot=lslot,
                    Ssb_i=Ssb_i: e.activation(
                        Ssb_i[:, hoff:hoff + W], Ssb_i[:, hoff:hoff + W],
                        AF.Exp, bias=0.0, scale=-1.0,
                        accum_out=stat[:, lslot:lslot + 1]),
                    waits=[("vs", b.cnt["vs"]), sev_marks.get(i)],
                    inc="ss")
                nm_prev[h] = ex
            if i + 1 < NTI:
                emit_smm_evacs(i + 1, tr_m_prev)
            tr_m_prev = [transp_end[0], transp_end[1]]
            prev_i = i
        emit_transp_av(prev_i)
        r1_ = b.emit("vector", lambda e: e.reciprocal(
            stat[:, 0:16], stat[:, 0:16]),
            waits=[nm_prev[0], nm_prev[1]], inc="vs")
        r2x = b.emit("vector", lambda e: e.reciprocal(
            stat[:, 48:64], stat[:, 48:64]), hard=[r1_], inc="vs")
        lt1 = b.emit("tensor", lambda e: e.transpose(
            P[4][0:16, 0:128], stat[:, 0:16], identf[:]),
            waits=[r2x, wo_busy.get(id(P[4])), pe_busy2.get(id(P[4])),
                   sp_busy.get(id(P[4]))], inc="ts")
        lt2 = b.emit("tensor", lambda e: e.transpose(
            P[5][0:16, 0:128], stat[:, 48:64], identf[:]),
            waits=[avp_busy, sp_busy.get(id(P[5]))], inc="ts")
        le1 = b.emit("vector", lambda e: e.tensor_copy(
            ltrb[0:16, 0:128], P[4][0:16, 0:128]), waits=[lt2], inc="vs")
        le2 = b.emit("vector", lambda e: e.tensor_copy(
            ltrb[0:16, 128:256], P[5][0:16, 0:128]), hard=[le1], inc="vs")
        lev = None
        for ti in range(NTI):
            pb = [P[0], P[1], P[2], P[3]][ti % 4]
            lmm = None
            for h in range(2):
                lmm = b.emit(
                    "tensor", lambda e, ti=ti, h=h, pb=pb: e.matmul(
                        pb[h * 64:(h + 1) * 64, 0:128],
                        sel16[0:16, ti * 128:ti * 128 + 64],
                        ltrb[0:16, h * 128:(h + 1) * 128],
                        start=True, stop=True, tile_position=(0, h * 64)),
                    waits=[le2, consts_done, sp_busy.get(id(pb))], inc="ts")
            lev = b.emit("vector", lambda e, ti=ti, pb=pb: e.tensor_copy(
                lbc[:, ti * 128:(ti + 1) * 128], pb[:, 0:128]),
                waits=[lmm], inc="vs")
            sp_busy[id(pb)] = lev
        epi_mark[0] = ("ts", b.cnt["ts"])
        lsc = b.emit("vector", lambda e: e.tensor_tensor(
            oT[:], oT[:], lbc[:], ALU.mult),
            waits=[("ss", b.cnt["ss"])], hard=[lev], inc="vs")
        lsc_prev[0] = lsc
        wop = [P[3], P[4]]
        for mc in range(NKC):
            for tt in range(NTT):
                pb = wop[(mc * NTT + tt) % 2]
                wts = [avp_busy, wst["o"], wo_busy.get(id(pb)),
                       pe_busy2.get(id(pb)), sp_busy.get(id(pb)), lsc]
                mm = b.emit("tensor", lambda e, mc=mc, tt=tt, pb=pb: e.matmul(
                    pb[:], wos[:, mc * 128:(mc + 1) * 128],
                    oT[:, tt * TT:(tt + 1) * TT], start=True, stop=True),
                    waits=wts, inc="ts")
                ad = b.emit("vector", lambda e, mc=mc, tt=tt, pb=pb:
                            e.tensor_tensor(
                                acc[:, mc * T + tt * TT:
                                    mc * T + tt * TT + TT],
                                acc[:, mc * T + tt * TT:
                                    mc * T + tt * TT + TT],
                                pb[:], ALU.add),
                            waits=[mm], inc="vs")
                wo_busy[id(pb)] = ad
                last_acc_add = ad
        wslice_free["o"] = ("ts", b.cnt["ts"])
    attn_done = b.snap()
    if debug:
        b.dma("gpsimd", dbg["dbg_acc_p2"][:, :], acc[:], waits=attn_done)
        b.dma("gpsimd", dbg["dbg_qb"][:, :], qb[:], waits=attn_done)
        b.dma("gpsimd", dbg["dbg_kb"][:, :], kb[:], waits=attn_done)
        b.dma("gpsimd", dbg["dbg_vb"][:, :], vb[:], waits=attn_done)
        b.dma("gpsimd", dbg["dbg_oT"][:, :], oT[:], waits=attn_done)

    trmA = b.emit("tensor", lambda e: e.transpose(
        P[4][0:16, 0:128], stat[:, 16:32], identf[:]),
        waits=attn_done + [ps_busy.get(id(P[4])), wo_busy.get(id(P[4]))],
        inc="ts")
    trmB = b.emit("tensor", lambda e: e.transpose(
        P[5][0:16, 0:128], stat[:, 32:48], identf[:]),
        waits=[ps_busy.get(id(P[5])), wo_busy.get(id(P[5]))], inc="ts")
    tevA = b.emit("vector", lambda e: e.tensor_copy(
        trscb[0:16, 0:128], P[4][0:16, 0:128]), waits=[trmA], inc="vs")
    tevB = b.emit("vector", lambda e: e.tensor_copy(
        trscb[0:16, 128:256], P[5][0:16, 0:128]), waits=[trmB], inc="vs")
    ps_busy[id(P[4])] = tevA
    ps_busy[id(P[5])] = tevB
    bc_last = None
    for ti in range(NTI):
        pb = P[4 + (ti % 2)]
        mmM = b.emit("tensor", lambda e, ti=ti, pb=pb: e.matmul(
            pb[:, 0:128], sel16[0:16, ti * 128:(ti + 1) * 128],
            trscb[0:16, 0:128], start=True, stop=True),
            waits=[tevA, tevB, consts_done, ps_busy.get(id(pb)),
                   wo_busy.get(id(pb))], inc="ts")
        mmR = b.emit("tensor", lambda e, ti=ti, pb=pb: e.matmul(
            pb[:, 128:256], sel16[0:16, ti * 128:(ti + 1) * 128],
            trscb[0:16, 128:256], start=True, stop=True), inc="ts")
        evM = b.emit("vector", lambda e, ti=ti, pb=pb: e.tensor_copy(
            mu2bc[:, ti * 128:(ti + 1) * 128], pb[:, 0:128]),
            waits=[mmM], inc="vs")
        evR = b.emit("vector", lambda e, ti=ti, pb=pb: e.tensor_copy(
            r2bc[:, ti * 128:(ti + 1) * 128], pb[:, 128:256]),
            waits=[mmR], inc="vs")
        ps_busy[id(pb)] = evR
        bc_last = evR
    xh_done = []
    tprev = None
    for kc in range(NKC):
        t1 = b.emit("vector", lambda e, kc=kc: e.tensor_tensor(
            xh_tmp[:], xnT[:, kc * T:(kc + 1) * T], mu2bc[:], ALU.subtract),
            waits=[bc_last], hard=[tprev], inc="vs")
        t2 = b.emit("vector", lambda e, kc=kc: e.tensor_tensor(
            xhatT[:, kc * T:(kc + 1) * T], xh_tmp[:], r2bc[:], ALU.mult),
            hard=[t1], inc="vs")
        tprev = t2
        xh_done.append(t2)
    ccast = b.dma("gpsimd", centb[:], dcent[:, :], waits=attn_done,
                  chain="cs")
    trc = None
    for kc in range(NKC):
        trc = b.emit("tensor", lambda e, kc=kc: e.transpose(
            PT0[:, kc * 8:(kc + 1) * 8], centb[:, kc * 128:(kc + 1) * 128],
            identb[0:8, 0:8]),
            waits=[ccast, pt_busy.get(id(PT0))], inc="ts")
    cev = b.emit("vector", lambda e: e.tensor_copy(centT[:], PT0[:, 0:64]),
                 waits=[trc], inc="vs")
    pt_busy[id(PT0)] = cev
    aff_last = None
    for tt in range(NTT):
        pb = P[4 + (tt % 2)]
        mm = None
        for kc in range(NKC):
            mm = b.emit("tensor", lambda e, kc=kc, tt=tt, pb=pb: e.matmul(
                pb[0:8, :], centT[:, kc * 8:(kc + 1) * 8],
                xnT[:, kc * T + tt * TT: kc * T + tt * TT + TT],
                start=(kc == 0), stop=(kc == NKC - 1)),
                waits=[cev, ps_busy.get(id(pb))], inc="ts")
        ev = b.emit("vector", lambda e, tt=tt, pb=pb: e.tensor_copy(
            affem[:, tt * TT:(tt + 1) * TT], pb[0:8, :]), waits=[mm],
            inc="vs")
        ps_busy[id(pb)] = ev
        aff_last = ev
    atrd = None
    for g in range(2):
        pb = P[4 + g]
        tr = None
        for j in range(8):
            ti = g * 8 + j
            tr = b.emit("tensor", lambda e, ti=ti, j=j, pb=pb: e.transpose(
                pb[:, j * 8:(j + 1) * 8], affem[:, ti * 128:(ti + 1) * 128],
                identf[0:8, 0:8]),
                waits=[aff_last, ps_busy.get(id(pb))], inc="ts")
        ev = b.emit("vector", lambda e, g=g, pb=pb: e.tensor_copy(
            afftm[:, g * 64:(g + 1) * 64], pb[:, 0:64]), waits=[tr], inc="vs")
        ps_busy[id(pb)] = ev
        atrd = ev
    pvc = [None]

    def vch2(fn, waits=()):
        pvc[0] = b.emit("vector", fn, waits=waits, hard=[pvc[0]], inc="vs")
        return pvc[0]

    mc_ = vch2(lambda e: e.tensor_reduce(
        stat[:, 48:64], afftm[:].rearrange("p (ti e) -> p ti e", e=8),
        AX.X, ALU.max), waits=[atrd])
    for ti in range(NTI):
        vch2(lambda e, ti=ti: e.tensor_scalar(
            ismax[:, ti * 8:(ti + 1) * 8], afftm[:, ti * 8:(ti + 1) * 8],
            stat[:, 48 + ti:49 + ti], None, ALU.is_ge))
    for ti in range(NTI):
        vch2(lambda e, ti=ti: e.tensor_tensor(
            ismax[:, ti * 8:(ti + 1) * 8], ismax[:, ti * 8:(ti + 1) * 8],
            c916[:], ALU.mult), waits=[consts_done])
    vch2(lambda e: e.tensor_scalar(
        ismax[:], ismax[:], -1.0, 9.0, ALU.mult, ALU.add))
    em_ = vch2(lambda e: e.tensor_reduce(
        stat[:, 64:80], ismax[:].rearrange("p (ti e) -> p ti e", e=8),
        AX.X, ALU.min))
    msk_last = None
    for ti in range(NTI):
        msk_last = vch2(lambda e, ti=ti: e.tensor_scalar(
            ismax[:, ti * 8:(ti + 1) * 8], iota8[:],
            stat[:, 64 + ti:65 + ti], None, ALU.is_equal))
    alp = b.emit("scalar", lambda e: e.activation(
        affem[:], affem[:], AF.Sigmoid),
        waits=[atrd], inc="ss")
    mtr = None
    for g in range(4):
        pb = P[4 + (g % 2)]
        tr = None
        for j in range(4):
            ti = g * 4 + j
            tr = b.emit("tensor", lambda e, ti=ti, j=j, pb=pb: e.transpose(
                pb[0:8, j * 128:(j + 1) * 128],
                ismax[:, ti * 8:(ti + 1) * 8], identf[:]),
                waits=[msk_last, ps_busy.get(id(pb))], inc="ts")
        ev = b.emit("vector", lambda e, g=g, pb=pb: e.tensor_copy(
            maskem[:, g * 512:(g + 1) * 512], pb[0:8, 0:512]),
            waits=[tr], inc="vs")
        ps_busy[id(pb)] = ev
        mtr = ev
    wem_done = b.emit("vector", lambda e: e.tensor_tensor(
        w_em[:], maskem[:], affem[:], ALU.mult), waits=[alp],
        hard=[mtr], inc="vs")
    selc = b.dma("sync", selb[:], dc["c_sel"][:, :], waits=attn_done,
                 chain="sel")
    p25_done = b.snap()
    if debug:
        b.dma("gpsimd", dbg["dbg_xhatT"][:, :], xhatT[:], waits=p25_done)
        b.dma("gpsimd", dbg["dbg_wem"][:, :], w_em[:], waits=p25_done)
        b.dma("gpsimd", dbg["dbg_maskem"][:, :], maskem[:], waits=p25_done)
        b.dma("gpsimd", dbg["dbg_alpha"][:, :], affem[:], waits=p25_done)

    w1r = dw1.rearrange("e (kc p) c -> e kc p c", p=128)
    w2r = dw2.rearrange("e (fc p) c -> e fc p c", p=128)
    TPS = [P[0], P[1]]
    ZPS = [P[2], P[3]]
    tps_busy = {}
    zps_busy = {}
    w1qs, w2qs = [w1q0, w1q1], [w2q0, w2q1]
    hbufs = [hbuf0, hbuf1]
    b1ts, gamTs, betTs = [b1t0, b1t1], [gamT0, gamT1], [betT0, betT1]
    pb1s, pgms, pbts = [pb1a, pb1b], [pgma, pgmb], [pbta, pbtb]
    QF = 8
    NSTEP = NE * 4
    NGT = NSTEP * 4

    def stream_w1_q(e_, q, buf, extra_waits):
        return b.dma("gpsimd",
                     w1qs[buf][:].rearrange("p (kc c) -> p kc c", kc=NKC),
                     w1r[e_, :, :, q * 1024:(q + 1) * 1024]
                     .rearrange("kc p c -> p kc c"),
                     waits=extra_waits, chain="w1")

    def stream_w2_q(e_, q, buf, extra_waits):
        return b.dma("gpsimd",
                     w2qs[buf][:].rearrange("p (fc c) -> p fc c", fc=QF),
                     w2r[e_, q * QF:(q + 1) * QF]
                     .rearrange("fc p c -> p fc c"),
                     waits=extra_waits, chain="w2")

    pstate = {}

    def param_dmas(e_, extra_waits):
        pi = e_ % 2
        prev = pstate.get(e_ - 2, {})
        st = {}
        st["l1"] = b.dma("gpsimd", pb1s[pi][:],
                         db1.rearrange("e (a c) -> e a c", c=128)[e_],
                         waits=list(extra_waits) + [prev.get("tb1")],
                         chain="pp")
        st["lg"] = b.dma("gpsimd", pgms[pi][:],
                         dgam.rearrange("e (a c) -> e a c", c=128)[e_],
                         waits=[prev.get("tgb")], chain="pp")
        st["lb"] = b.dma("gpsimd", pbts[pi][:],
                         dbet.rearrange("e (a c) -> e a c", c=128)[e_],
                         waits=[prev.get("tgb")], chain="pp")
        pstate[e_] = st

    def param_compute(e_, extra_waits):
        pi = e_ % 2
        st = pstate[e_]
        prev = pstate.get(e_ - 2, {})
        t1 = b.emit("tensor", lambda e, pi=pi: e.transpose(
            P[4][:, 0:32], pb1s[pi][:], identf[0:32, 0:32]),
            waits=list(extra_waits) + [st["l1"], ps_busy.get(id(P[4]))],
            inc="ts")
        st["tb1"] = ("ts", b.cnt["ts"])
        e1 = b.emit("vector", lambda e, pi=pi: e.tensor_copy(
            b1ts[pi][:], P[4][:, 0:32]),
            waits=[t1, prev.get("rl_last")], inc="vs")
        ps_busy[id(P[4])] = e1
        st["b1"] = e1
        t3 = b.emit("tensor", lambda e, pi=pi: e.transpose(
            P[5][:, 0:8], pgms[pi][:], identf[0:8, 0:8]),
            waits=[st["lg"], ps_busy.get(id(P[5]))], inc="ts")
        t4 = b.emit("tensor", lambda e, pi=pi: e.transpose(
            P[5][:, 8:16], pbts[pi][:], identf[0:8, 0:8]),
            waits=[st["lb"]], inc="ts")
        st["tgb"] = ("ts", b.cnt["ts"])
        b.emit("vector", lambda e, pi=pi: e.tensor_copy(
            gamTs[pi][:], P[5][:, 0:8]), waits=[t4], inc="vs")
        e4 = b.emit("vector", lambda e, pi=pi: e.tensor_copy(
            betTs[pi][:], P[5][:, 8:16]), inc="vs")
        ps_busy[id(P[5])] = e4
        st["gb"] = e4

    def emit_wbc(e_):
        wbc_done = None
        for tt in range(NTT):
            pb = P[4] if (tt % 2 == 0) else P[5]
            mm = b.emit("tensor", lambda e, e_=e_, tt=tt, pb=pb: e.matmul(
                pb[:], selb[:, e_ * 128:(e_ + 1) * 128],
                w_em[:, tt * TT:(tt + 1) * TT], start=True, stop=True),
                waits=[selc, wem_done, ps_busy.get(id(pb)),
                       zps_busy.get(id(pb))], inc="ts")
            ev = b.emit("vector", lambda e, tt=tt, pb=pb: e.tensor_copy(
                wbcb[:, tt * TT:(tt + 1) * TT], pb[:]),
                waits=[mm], inc="vs")
            ps_busy[id(pb)] = ev
            wbc_done = ev
        pstate[e_]["wbc"] = wbc_done

    zlast = [None]
    hbuf_busy = [None, None]
    tbuf_busy = [None]
    w1q_busy = [None, None]
    w2q_busy = [None, None]
    hd_done = {}

    def emit_hd(gti, extra=()):
        e_ = gti // 16
        tt = gti % 4
        pi2 = e_ % 2
        hb = hbufs[gti % 2]
        last = None
        for kc in range(NKC):
            last = b.emit(
                "vector",
                lambda e, kc=kc, tt=tt, pi2=pi2, hb=hb: e.tensor_scalar(
                    hb[:, kc * TT:(kc + 1) * TT],
                    xhatT[:, kc * T + tt * TT: kc * T + tt * TT + TT],
                    gamTs[pi2][:, kc:kc + 1], betTs[pi2][:, kc:kc + 1],
                    ALU.mult, ALU.add),
                waits=list(extra) + [xh_done[kc], hbuf_busy[gti % 2]],
                inc="vs")
        hd_done[gti] = last

    w1_ready = [None, None]
    w2_ready = [None, None]
    w1_ready[0] = stream_w1_q(0, 0, 0, p25_done)
    w2_ready[0] = stream_w2_q(0, 0, 0, p25_done)
    w1_ready[1] = stream_w1_q(0, 1, 1, p25_done)
    w2_ready[1] = stream_w2_q(0, 1, 1, p25_done)
    param_dmas(0, p25_done)
    param_compute(0, p25_done)
    emit_wbc(0)
    param_dmas(1, [])
    b2ld = b.dma("gpsimd", b2all[:], db2[:, :], waits=p25_done, chain="pp")
    emit_hd(0, p25_done)

    for s in range(NSTEP):
        e_, q = divmod(s, 4)
        buf = s % 2
        pi = e_ % 2
        est = pstate[e_]
        for tt in range(NTT):
            gti = s * 4 + tt
            hb = hbufs[gti % 2]
            tb = tbufq
            relu_ms = {}
            for fcl in range(QF):
                fc = q * QF + fcl
                bank = TPS[fcl % 2]
                t_mm_last = None
                for kc in range(NKC):
                    wts = [hd_done[gti], w1_ready[buf]]
                    if kc == 0:
                        wts.append(tps_busy.get(id(bank)))
                    t_mm_last = b.emit(
                        "tensor",
                        lambda e, kc=kc, fcl=fcl, bank=bank, buf=buf, hb=hb:
                        e.matmul(
                            bank[:],
                            w1qs[buf][:].rearrange("p (kc c) -> p kc c",
                                                   kc=NKC)
                            [:, kc, fcl * 128:(fcl + 1) * 128],
                            hb[:, kc * TT:(kc + 1) * TT],
                            start=(kc == 0), stop=(kc == NKC - 1)),
                        waits=wts, inc="ts")
                wrl = [t_mm_last, est["b1"], tbuf_busy[0]]
                if s == 0 and tt == 0 and fcl == 0:
                    wrl += p25_done
                rl = b.emit(
                    "scalar",
                    lambda e, fcl=fcl, fc=fc, bank=bank, pi=pi, tb=tb:
                    e.activation(
                        tb[:, fcl * TT:(fcl + 1) * TT], bank[:],
                        AF.Relu, bias=b1ts[pi][:, fc:fc + 1], scale=1.0),
                    waits=wrl, inc="ss")
                tps_busy[id(bank)] = rl
                relu_ms[fcl] = rl
            hbuf_busy[gti % 2] = ("ts", b.cnt["ts"])
            if gti + 1 < NGT:
                emit_hd(gti + 1)
            for mc in range(NKC):
                bank = ZPS[mc % 2]
                z_mm_last = None
                for fcl in range(QF):
                    wts = [relu_ms[fcl], w2_ready[buf]]
                    if fcl == 0:
                        wts.append(zps_busy.get(id(bank)))
                    z_mm_last = b.emit(
                        "tensor",
                        lambda e, fcl=fcl, mc=mc, bank=bank, buf=buf, tb=tb:
                        e.matmul(
                            bank[:],
                            w2qs[buf][:].rearrange("p (fc c) -> p fc c",
                                                   fc=QF)
                            [:, fcl, mc * 128:(mc + 1) * 128],
                            tb[:, fcl * TT:(fcl + 1) * TT],
                            start=(fcl == 0),
                            stop=(fcl == QF - 1),
                            skip_group_check=True),
                        waits=wts, inc="ts")
                zm = b.emit(
                    "vector",
                    lambda e, tt=tt, bank=bank: e.tensor_tensor(
                        ztmp[:], bank[:],
                        wbcb[:, tt * TT:(tt + 1) * TT], ALU.mult),
                    waits=[z_mm_last],
                    hard=[est["wbc"], zlast[0]], inc="vs")
                zps_busy[id(bank)] = zm
                zlast[0] = b.emit(
                    "vector",
                    lambda e, mc=mc, tt=tt: e.tensor_tensor(
                        acc[:, mc * T + tt * TT: mc * T + tt * TT + TT],
                        acc[:, mc * T + tt * TT: mc * T + tt * TT + TT],
                        ztmp[:], ALU.add),
                    waits=[last_acc_add], hard=[zm], inc="vs")
            tbuf_busy[0] = ("ts", b.cnt["ts"])
        w1q_busy[buf] = ("ts", b.cnt["ts"])
        w2q_busy[buf] = ("ts", b.cnt["ts"])
        if s + 2 < NSTEP:
            e2_, q2 = divmod(s + 2, 4)
            w1_ready[buf] = stream_w1_q(e2_, q2, buf, [w1q_busy[buf]])
            w2_ready[buf] = stream_w2_q(e2_, q2, buf, [w2q_busy[buf]])
        if q == 0 and 2 <= e_ + 1 < NE:
            param_dmas(e_ + 1, [])
        if q == 2 and e_ + 1 < NE:
            param_compute(e_ + 1, [])
        if q == 3:
            pstate[e_]["rl_last"] = ("ss", b.cnt["ss"])
            if e_ + 1 < NE:
                emit_wbc(e_ + 1)
    for tt in range(NTT):
        for mc in range(NKC):
            bank = ZPS[mc % 2]
            mm = b.emit(
                "tensor",
                lambda e, mc=mc, tt=tt, bank=bank: e.matmul(
                    bank[:], b2all[:, mc * 128:(mc + 1) * 128],
                    w_em[:, tt * TT:(tt + 1) * TT], start=True, stop=True),
                waits=[b2ld, wem_done, zps_busy.get(id(bank))], inc="ts")
            ad = b.emit(
                "vector",
                lambda e, mc=mc, tt=tt, bank=bank: e.tensor_tensor(
                    acc[:, mc * T + tt * TT: mc * T + tt * TT + TT],
                    acc[:, mc * T + tt * TT: mc * T + tt * TT + TT],
                    bank[:], ALU.add),
                waits=[mm], inc="vs")
            zps_busy[id(bank)] = ad
            zlast[0] = ad
    ffn_done = b.snap()

    outb = [out_tm0, out_tm1]
    ost_busy = [None, None]
    final_dma = None
    last_store = [None, None]
    for ti in range(NTI):
        ob = outb[ti % 2]
        evs = None
        for half in range(2):
            pb = P[4 + half]
            tr = None
            for j in range(4):
                kc = half * 4 + j
                tr = b.emit("tensor", lambda e, kc=kc, j=j, ti=ti, pb=pb:
                            e.transpose(
                                pb[:, j * 128:(j + 1) * 128],
                                acc[:, kc * T + ti * 128:
                                    kc * T + ti * 128 + 128], identf[:]),
                            waits=ffn_done + [ps_busy.get(id(pb))],
                            inc="ts")
            ev = b.emit("vector", lambda e, half=half, ob=ob, pb=pb:
                        e.tensor_copy(
                            ob[:, half * 512:(half + 1) * 512], pb[:]),
                        waits=[tr, ost_busy[ti % 2]], inc="vs")
            ps_busy[id(pb)] = ev
            evs = ev
        st = b.dma("gpsimd", dout.rearrange("(ti p) c -> ti p c", p=128)[ti],
                   ob[:], waits=[evs], chain="o%d" % (ti % 2))
        ost_busy[ti % 2] = st
        final_dma = st
        last_store[ti % 2] = st
    b.emit("gpsimd", lambda e: e.nop(),
           waits=[last_store[0], last_store[1]])

    with nc.Block() as block:
        semstack = b.finalize(block)
    semstack.close()
    ovstack.close()
    al.close()
    return nc


def kernel(L0_attn_x, L0_attn_Wq, L0_attn_Wk, L0_attn_Wv, L0_attn_Wo,
           L0_ffn_centroids, L0_ffn_w1, L0_ffn_b1, L0_ffn_w2, L0_ffn_b2,
           L0_ffn_gamma, L0_ffn_beta):
    x = np.ascontiguousarray(np.asarray(L0_attn_x, dtype=np.float32))
    assert x.shape == (8, T, D)
    shared = {
        "L0_attn_Wq": np.ascontiguousarray(L0_attn_Wq, dtype=np.float32),
        "L0_attn_Wk": np.ascontiguousarray(L0_attn_Wk, dtype=np.float32),
        "L0_attn_Wv": np.ascontiguousarray(L0_attn_Wv, dtype=np.float32),
        "L0_attn_Wo": np.ascontiguousarray(L0_attn_Wo, dtype=np.float32),
        "L0_ffn_centroids": np.ascontiguousarray(L0_ffn_centroids,
                                                 dtype=np.float32),
        "L0_ffn_w1": np.ascontiguousarray(L0_ffn_w1, dtype=np.float32),
        "L0_ffn_b1": np.ascontiguousarray(L0_ffn_b1, dtype=np.float32),
        "L0_ffn_w2": np.ascontiguousarray(L0_ffn_w2, dtype=np.float32),
        "L0_ffn_b2": np.ascontiguousarray(L0_ffn_b2, dtype=np.float32),
        "L0_ffn_gamma": np.ascontiguousarray(L0_ffn_gamma, dtype=np.float32),
        "L0_ffn_beta": np.ascontiguousarray(L0_ffn_beta, dtype=np.float32),
    }
    shared.update(make_consts())
    nc = build_program()
    in_maps = []
    for c in range(8):
        m = dict(shared)
        m["L0_attn_x"] = np.ascontiguousarray(x[c])
        in_maps.append(m)
    res = run_bass_kernel_spmd(nc, in_maps, core_ids=list(range(8)))
    return np.stack([np.asarray(r["out"], dtype=np.float32)
                     for r in res.results], axis=0)


# version 32 (final)
# speedup vs baseline: 1.2141x; 1.0000x over previous
import contextlib
import numpy as np
import ml_dtypes
import concourse.bass as bass
import concourse.mybir as mybir
from concourse.bass_utils import run_bass_kernel_spmd

F32 = mybir.dt.float32
BF16 = mybir.dt.bfloat16
AF = mybir.ActivationFunctionType
ALU = mybir.AluOpType
AX = mybir.AxisListType

ENGINES = ["sync", "scalar", "vector", "tensor", "gpsimd"]
COMPUTE_SEMS = {"ts": "tensor", "vs": "vector", "ss": "scalar", "gs": "gpsimd"}

D = 1024
T = 2048
NKC = 8
TT = 512
NTT = T // TT
NTI = 16
DK = 64
NE = 8
HALF_FC = 16
BIG = 1.0e30
EPS = 1.0e-5


class Builder:
    def __init__(self, nc):
        self.nc = nc
        self.ops = {e: [] for e in ENGINES}
        self.cnt = {}
        self.maxwait = {}
        self.sem_names = []

    def sem(self, name):
        if name not in self.cnt:
            self.cnt[name] = 0
            self.sem_names.append(name)
        return name

    def emit(self, eng, fn, waits=(), inc=None, inc_amt=1, hard=()):
        w = []
        for (soft, wl) in ((True, waits), (False, hard)):
            for x in wl:
                if x is None:
                    continue
                (s, v) = x
                if v <= 0:
                    continue
                key = (eng, s)
                if self.maxwait.get(key, -1) >= v:
                    continue
                self.maxwait[key] = v
                w.append((s, v))
        post = None
        if inc is not None:
            self.sem(inc)
            self.cnt[inc] += inc_amt
            post = (inc, self.cnt[inc])
        self.ops[eng].append((fn, w, inc, inc_amt))
        return post

    def dma(self, queue, out, in_, waits=(), chain=None):
        sem = "q_" + (chain if chain is not None else queue)
        self.sem(sem)
        prev = (sem, self.cnt[sem])
        return self.emit(queue, lambda e, o=out, i=in_: e.dma_start(o, i),
                         waits=list(waits) + [prev], inc=sem, inc_amt=16)

    def snap(self):
        return [(s, self.cnt.get(s, 0)) for s in ("ts", "vs", "ss", "gs")]

    def replay(self, eng_name, eng, sems):
        for (fn, waits, inc, amt) in self.ops[eng_name]:
            for (s, v) in waits:
                eng.wait_ge(sems[s], v)
            inst = fn(eng)
            if inc is not None:
                inst.then_inc(sems[inc], amt)

    def finalize(self, block):
        stack = contextlib.ExitStack()
        sems = {}
        for name in self.sem_names:
            sems[name] = stack.enter_context(self.nc.semaphore(name))
        for e in ENGINES:
            getattr(block, e)(lambda eng, n=e: self.replay(n, eng, sems))
        return stack


class Overlay:
    def __init__(self, nc, total_bytes):
        self.nc = nc
        self.total = total_bytes
        self.views = []

    def view(self, name, shape, dtype, off):
        per_part = int(np.prod(shape[1:])) * mybir.dt.size(dtype)
        assert off % 4 == 0 and off + per_part <= self.total, (
            name, off, per_part, self.total)
        h = self.nc.virtual_sbuf_tensor(name, shape, dtype)
        self.views.append((h, off))
        return h

    def allocate(self, stack, arena_name):
        arena = stack.enter_context(
            self.nc.sbuf_tensor(arena_name, [128, self.total], mybir.dt.uint8))
        base = self.nc.lookup_mloc(arena).addr
        for (h, off) in self.views:
            self.nc._tensor(h.name, h.shape, h.dtype, type="SB")
            mloc = self.nc.lookup_mloc(h)
            mloc.allocated = True
            mloc.addr = base + off


def make_consts():
    bf = ml_dtypes.bfloat16
    inv = 10000.0 ** (-np.arange(0, DK, 2) / DK)
    ang = np.outer(np.arange(T), inv)
    cos = np.repeat(np.cos(ang), 2, axis=1)
    sin = np.repeat(np.sin(ang), 2, axis=1)
    cos_fm = np.tile(cos.T, (2, 1)).astype(np.float32)
    sin_fm = np.tile(sin.T, (2, 1)).astype(np.float32)
    p = np.zeros((128, 128), np.float32)
    for j in range(64):
        p[2 * j + 1, 2 * j] = -1.0
        p[2 * j, 2 * j + 1] = 1.0
    ident = np.eye(128, dtype=np.float32)
    mask = np.zeros((128, 1024), np.float32)
    pidx = np.arange(128)
    bigp = 0.125 * BIG
    for c in range(384, 512):
        mask[:, c] = np.where((c - 384) <= pidx, 0.0, bigp)
    mask[:, 512:] = bigp
    iota8 = np.tile(np.arange(8, dtype=np.float32), (128, 1))
    c916 = np.tile(9.0 - np.arange(8, dtype=np.float32), (128, 1))
    sel = np.zeros((8, 8 * 128), np.float32)
    for e in range(8):
        sel[e, e * 128:(e + 1) * 128] = 1.0
    sel16 = np.zeros((16, 16 * 128), np.float32)
    for ti in range(16):
        sel16[ti, ti * 128:(ti + 1) * 128] = 1.0
    return {
        "c_cos": cos_fm.astype(bf), "c_sin": sin_fm.astype(bf),
        "c_p128": p.astype(bf), "c_ident": ident.astype(bf),
        "c_identf": ident, "c_mask": mask.astype(bf),
        "c_iota8": iota8, "c_c916": c916, "c_sel": sel.astype(bf),
        "c_sel16": sel16.astype(bf),
    }


def build_program(debug=False):
    nc = bass.Bass("TRN2", target_bir_lowering=False)
    b = Builder(nc)

    dx = nc.dram_tensor("L0_attn_x", [T, D], F32, kind="ExternalInput")
    dwq = nc.dram_tensor("L0_attn_Wq", [D, D], F32, kind="ExternalInput")
    dwk = nc.dram_tensor("L0_attn_Wk", [D, D], F32, kind="ExternalInput")
    dwv = nc.dram_tensor("L0_attn_Wv", [D, D], F32, kind="ExternalInput")
    dwo = nc.dram_tensor("L0_attn_Wo", [D, D], F32, kind="ExternalInput")
    dcent = nc.dram_tensor("L0_ffn_centroids", [NE, D], F32,
                           kind="ExternalInput")
    dw1 = nc.dram_tensor("L0_ffn_w1", [NE, D, 4096], F32,
                         kind="ExternalInput")
    db1 = nc.dram_tensor("L0_ffn_b1", [NE, 4096], F32, kind="ExternalInput")
    dw2 = nc.dram_tensor("L0_ffn_w2", [NE, 4096, D], F32,
                         kind="ExternalInput")
    db2 = nc.dram_tensor("L0_ffn_b2", [NE, D], F32, kind="ExternalInput")
    dgam = nc.dram_tensor("L0_ffn_gamma", [NE, D], F32, kind="ExternalInput")
    dbet = nc.dram_tensor("L0_ffn_beta", [NE, D], F32, kind="ExternalInput")
    dout = nc.dram_tensor("out", [T, D], F32, kind="ExternalOutput")
    drows = nc.dram_tensor("rows_scratch", [2, T], F32)
    dbg = {}
    if debug:
        for (n, shp, dt) in [("dbg_acc_p1", [128, NKC * T], F32),
                             ("dbg_xnT", [128, NKC * T], BF16),
                             ("dbg_stat", [128, 84], F32),
                             ("dbg_acc_p2", [128, NKC * T], F32),
                             ("dbg_xhatT", [128, NKC * T], BF16),
                             ("dbg_wem", [8, T], BF16),
                             ("dbg_maskem", [8, T], F32),
                             ("dbg_alpha", [8, T], F32),
                             ("dbg_qb", [128, T], BF16),
                             ("dbg_w1h", [128, NKC * HALF_FC * 128], BF16),
                             ("dbg_w2h", [128, HALF_FC * D], BF16),
                             ("dbg_tbuf", [128, HALF_FC * TT], BF16),
                             ("dbg_hbuf", [128, NKC * TT], BF16),
                             ("dbg_wbcb", [128, T], BF16),
                             ("dbg_b1t", [128, 32], F32),
                             ("dbg_gamT", [128, 8], F32),
                             ("dbg_betT", [128, 8], F32),
                             ("dbg_kb", [128, T], BF16),
                             ("dbg_vb", [128, T], BF16),
                             ("dbg_oT", [128, T], BF16)]:
            dbg[n] = nc.dram_tensor(n, shp, dt, kind="ExternalOutput")
    dc = {}
    for (n, shp, dt) in [("c_cos", [128, T], BF16), ("c_sin", [128, T], BF16),
                         ("c_p128", [128, 128], BF16),
                         ("c_ident", [128, 128], BF16),
                         ("c_identf", [128, 128], F32),
                         ("c_mask", [128, 1024], BF16),
                         ("c_iota8", [128, 8], F32),
                         ("c_c916", [128, 8], F32),
                         ("c_sel", [8, 8 * 128], BF16),
                         ("c_sel16", [16, 16 * 128], BF16)]:
        dc[n] = nc.dram_tensor(n, shp, dt, kind="ExternalInput")

    al = contextlib.ExitStack()

    acc = al.enter_context(nc.sbuf_tensor("acc", [128, NKC * T], F32))
    xhatT = al.enter_context(nc.sbuf_tensor("xhatT", [128, NKC * T], BF16))
    OVC = Overlay(nc, 1792)
    identb = OVC.view("identb", [128, 128], BF16, 0)
    identf = OVC.view("identf", [128, 128], F32, 512)
    iota8 = OVC.view("iota8", [128, 8], F32, 1024)
    c916 = OVC.view("c916", [128, 8], F32, 1056)
    onescol = OVC.view("onescol", [1, 128], BF16, 1088)
    stat = OVC.view("stat", [128, 84], F32, 1344)

    OVB = Overlay(nc, 32768)
    xnT = OVB.view("xnT", [128, NKC * T], BF16, 0)
    w1q0 = OVB.view("w1q0", [128, NKC * 8 * 128], BF16, 0)
    w1q1 = OVB.view("w1q1", [128, NKC * 8 * 128], BF16, 16384)

    OV = Overlay(nc, 78208)
    xstage = OV.view("xstage", [128, 2 * 1024], F32, 0)
    xnf = OV.view("xnf", [128, 1024], F32, 8192)
    xnb = OV.view("xnb", [128, 1024], BF16, 12288)
    xnf2 = OV.view("xnf2", [128, 1024], F32, 38912)
    xnb2 = OV.view("xnb2", [128, 1024], BF16, 43008)
    qb = OV.view("qb", [128, T], BF16, 0)
    kb = OV.view("kb", [128, T], BF16, 4096)
    vb = OV.view("vb", [128, T], BF16, 8192)
    qtmp = OV.view("qtmp", [128, 2 * TT], BF16, 12288)
    wqs = OV.view("wqs", [128, NKC * 128], BF16, 14336)
    wks = OV.view("wks", [128, NKC * 128], BF16, 16384)
    wvs = OV.view("wvs", [128, NKC * 128], BF16, 18432)
    wos = OV.view("wos", [128, 1024], BF16, 20480)
    S_p0 = OV.view("S_p0", [128, 2 * T], BF16, 22528)
    S_p1 = OV.view("S_p1", [128, 2 * T], BF16, 30720)
    A_sb = OV.view("A_sb", [128, 2 * T], BF16, 38912)
    AT_sb = OV.view("AT_sb", [128, T], BF16, 47104)
    oT = OV.view("oT", [128, T], BF16, 51200)
    cosb = OV.view("cosb", [128, T], BF16, 55296)
    sinb = OV.view("sinb", [128, T], BF16, 59392)
    maskm = OV.view("maskm", [128, 1024], BF16, 63488)
    p128 = OV.view("p128b", [128, 128], BF16, 65536)
    affem = OV.view("affem", [8, T], F32, 22528)
    afftm = OV.view("afftm", [128, 128], F32, 30720)
    ismax = OV.view("ismax", [128, 128], F32, 31232)
    maskem = OV.view("maskem", [8, T], F32, 31744)
    mu2bc = OV.view("mu2bc", [128, T], F32, 39936)
    r2bc = OV.view("r2bc", [128, T], F32, 48128)
    xh_tmp = OV.view("xh_tmp", [128, T], BF16, 56320)
    cent32 = OV.view("cent32", [8, 1024], F32, 60416)
    centb = OV.view("centb", [8, 1024], BF16, 64512)
    centT = OV.view("centT", [128, 64], BF16, 66560)
    trsc = OV.view("trsc", [32, 128], F32, 66688)
    rowsb = OV.view("rowsb", [1, 4096], BF16, 30720)
    w2q0 = OV.view("w2q0", [128, 8 * D], BF16, 0)
    w2q1 = OV.view("w2q1", [128, 8 * D], BF16, 16384)
    tbufq = OV.view("tbufq", [128, 8 * TT], BF16, 32768)
    hbuf0 = OV.view("hbuf0", [128, NKC * TT], BF16, 40960)
    hbuf1 = OV.view("hbuf1", [128, NKC * TT], BF16, 49152)
    ztmp = OV.view("ztmp", [128, TT], F32, 57344)
    wbcb = OV.view("wbcb", [128, T], BF16, 59392)
    w_em = OV.view("w_em", [8, T], BF16, 63488)
    b2all = OV.view("b2all", [8, 1024], BF16, 67584)
    pb1a = OV.view("pb1a", [32, 128], F32, 71680)
    pb1b = OV.view("pb1b", [32, 128], F32, 72192)
    pgma = OV.view("pgma", [8, 128], F32, 72704)
    pgmb = OV.view("pgmb", [8, 128], F32, 73216)
    pbta = OV.view("pbta", [8, 128], F32, 73728)
    pbtb = OV.view("pbtb", [8, 128], F32, 74240)
    b1t0 = OV.view("b1t0", [128, 32], F32, 74752)
    b1t1 = OV.view("b1t1", [128, 32], F32, 74880)
    gamT0 = OV.view("gamT0", [128, 8], F32, 75008)
    gamT1 = OV.view("gamT1", [128, 8], F32, 75040)
    betT0 = OV.view("betT0", [128, 8], F32, 75072)
    betT1 = OV.view("betT1", [128, 8], F32, 75104)
    selb = OV.view("selb", [8, 8 * 128], BF16, 76160)
    ltrb = OV.view("ltrb", [16, 256], BF16, 12288)
    lbc = OV.view("lbc", [128, T], BF16, 65792)
    trscb = OV.view("trscb", [16, 256], BF16, 60416)
    sel16 = OV.view("sel16", [16, 16 * 128], BF16, 69888)
    out_tm0 = OV.view("out_tm0", [128, 1024], F32, 0)
    out_tm1 = OV.view("out_tm1", [128, 1024], F32, 4096)

    P = [al.enter_context(nc.psum_tensor(f"P{i}", [128, 512], F32))
         for i in range(6)]
    PT0 = al.enter_context(nc.psum_tensor("PT0", [128, 1024], BF16))
    PT1 = al.enter_context(nc.psum_tensor("PT1", [128, 1024], BF16))
    PTs = [PT0, PT1]

    ovstack = contextlib.ExitStack()
    OVC.allocate(ovstack, "arenaC")
    OVB.allocate(ovstack, "arenaB")
    OV.allocate(ovstack, "arenaP")

    c_loads = []
    for (dst, src) in [(identb[:], dc["c_ident"][:, :]),
                       (identf[:], dc["c_identf"][:, :]),
                       (p128[:], dc["c_p128"][:, :]),
                       (iota8[:], dc["c_iota8"][:, :]),
                       (c916[:], dc["c_c916"][:, :]),
                       (maskm[:], dc["c_mask"][:, :]),
                       (cosb[:], dc["c_cos"][:, :]),
                       (sinb[:], dc["c_sin"][:, :]),
                       (sel16[:], dc["c_sel16"][:, :])]:
        c_loads.append(b.dma("gpsimd", dst, src, chain="cg"))
    ocs = b.emit("vector", lambda e: e.memset(onescol[:], 1.0), inc="vs")
    consts_done = c_loads[-1]

    xs = [xstage[:, 0:1024], xstage[:, 1024:2048]]
    xnfs = [xnf, xnf2]
    xnbs = [xnb, xnb2]
    psA = [P[4], P[5]]
    ps_busy = {}
    pt_busy = {}
    par_done = [None, None]
    xnb_free = [None, None]
    ln_loads = [b.dma("gpsimd", xs[ti % 2],
                      dx.rearrange("(ti p) c -> ti p c", p=128)[ti],
                      waits=[consts_done], chain="x%d" % (ti % 2))
                for ti in range(2)]
    for ti in range(NTI):
        pi2 = ti % 2
        sb = 48 * pi2
        xbuf = xs[pi2]
        xnf_t = xnfs[pi2]
        xnb_t = xnbs[pi2]
        w0 = [ln_loads[ti], par_done[pi2]]
        pv = [None]

        def vch(fn, waits=()):
            pv[0] = b.emit("vector", fn, waits=waits, hard=[pv[0]], inc="vs")
            return pv[0]

        vch(lambda e, xbuf=xbuf, sb=sb: e.reduce_sum(
            stat[:, sb + 0:sb + 1], xbuf, AX.X), waits=w0)
        vch(lambda e, sb=sb: e.tensor_scalar(
            stat[:, sb + 1:sb + 2], stat[:, sb + 0:sb + 1], 1.0 / D, None,
            ALU.mult))
        sq = b.emit("scalar", lambda e, xbuf=xbuf, sb=sb, xnf_t=xnf_t:
                    e.activation(
                        xnf_t[:], xbuf, AF.Square,
                        accum_out=stat[:, sb + 2:sb + 3]),
                    waits=w0, inc="ss")
        vch(lambda e, sb=sb: e.tensor_scalar(
            stat[:, sb + 3:sb + 4], stat[:, sb + 2:sb + 3], 1.0 / D, None,
            ALU.mult), waits=[sq])
        vch(lambda e, sb=sb: e.tensor_tensor(
            stat[:, sb + 4:sb + 5], stat[:, sb + 1:sb + 2],
            stat[:, sb + 1:sb + 2], ALU.mult))
        vch(lambda e, sb=sb: e.tensor_tensor(
            stat[:, sb + 4:sb + 5], stat[:, sb + 3:sb + 4],
            stat[:, sb + 4:sb + 5], ALU.subtract))
        ve = vch(lambda e, sb=sb: e.tensor_scalar(
            stat[:, sb + 5:sb + 6], stat[:, sb + 4:sb + 5], EPS, None,
            ALU.add))
        sz = b.emit("scalar", lambda e, sb=sb: e.activation(
            stat[:, sb + 6:sb + 7], stat[:, sb + 5:sb + 6], AF.Sqrt),
            waits=[ve], inc="ss")
        vch(lambda e, sb=sb: e.reciprocal(
            stat[:, sb + 7:sb + 8], stat[:, sb + 6:sb + 7]), waits=[sz])
        xnfo = vch(lambda e, xbuf=xbuf, sb=sb, xnf_t=xnf_t: e.tensor_scalar(
            xnf_t[:], xbuf, stat[:, sb + 1:sb + 2], stat[:, sb + 7:sb + 8],
            ALU.subtract, ALU.mult), waits=[sq])
        vch(lambda e, sb=sb, xnf_t=xnf_t: e.reduce_sum(
            stat[:, sb + 8:sb + 9], xnf_t[:], AX.X))
        vch(lambda e, ti=ti, sb=sb: e.tensor_scalar(
            stat[:, 16 + ti:17 + ti], stat[:, sb + 8:sb + 9], 1.0 / D, None,
            ALU.mult))
        sq2 = b.emit("scalar", lambda e, sb=sb, xnf_t=xnf_t, xnb_t=xnb_t:
                     e.activation(
                         xnb_t[:], xnf_t[:], AF.Square,
                         accum_out=stat[:, sb + 9:sb + 10]),
                     waits=[xnfo, xnb_free[pi2]], inc="ss")
        vch(lambda e, sb=sb: e.tensor_scalar(
            stat[:, sb + 10:sb + 11], stat[:, sb + 9:sb + 10], 1.0 / D, None,
            ALU.mult), waits=[sq2])
        vch(lambda e, ti=ti, sb=sb: e.tensor_tensor(
            stat[:, sb + 11:sb + 12], stat[:, 16 + ti:17 + ti],
            stat[:, 16 + ti:17 + ti], ALU.mult))
        vch(lambda e, sb=sb: e.tensor_tensor(
            stat[:, sb + 11:sb + 12], stat[:, sb + 10:sb + 11],
            stat[:, sb + 11:sb + 12], ALU.subtract))
        v2 = vch(lambda e, sb=sb: e.tensor_scalar(
            stat[:, sb + 12:sb + 13], stat[:, sb + 11:sb + 12], EPS, None,
            ALU.add))
        sz2 = b.emit("scalar", lambda e, sb=sb: e.activation(
            stat[:, sb + 13:sb + 14], stat[:, sb + 12:sb + 13], AF.Sqrt),
            waits=[v2], inc="ss")
        vch(lambda e, ti=ti, sb=sb: e.reciprocal(
            stat[:, 32 + ti:33 + ti], stat[:, sb + 13:sb + 14]),
            waits=[sz2])
        xnbo = vch(lambda e, xnf_t=xnf_t, xnb_t=xnb_t: e.tensor_copy(
            xnb_t[:], xnf_t[:]), waits=[sq2])
        vch(lambda e, sb=sb: e.tensor_scalar(
            stat[:, sb + 14:sb + 15], stat[:, sb + 7:sb + 8], 1.0, None,
            ALU.add))
        vch(lambda e, sb=sb: e.tensor_tensor(
            stat[:, sb + 15:sb + 16], stat[:, sb + 1:sb + 2],
            stat[:, sb + 7:sb + 8], ALU.mult))
        uo = vch(lambda e, xbuf=xbuf, sb=sb: e.tensor_scalar(
            xbuf, xbuf, stat[:, sb + 14:sb + 15], stat[:, sb + 15:sb + 16],
            ALU.mult, ALU.subtract))
        for half in range(2):
            pb = psA[half]
            tr = None
            for j in range(4):
                kc = half * 4 + j
                tr = b.emit("tensor", lambda e, kc=kc, j=j, pb=pb, xbuf=xbuf:
                            e.transpose(pb[:, j * 128:(j + 1) * 128],
                                        xbuf[:, kc * 128:(kc + 1) * 128],
                                        identf[:]),
                            waits=[uo, consts_done, ps_busy.get(id(pb))],
                            inc="ts")
            ev = b.emit("vector", lambda e, half=half, ti=ti, pb=pb:
                        e.tensor_copy(
                            acc[:].rearrange("p (kc t) -> p kc t", kc=NKC)
                            [:, half * 4:(half + 1) * 4,
                             ti * 128:(ti + 1) * 128], pb[:]),
                        waits=[tr], inc="vs")
            ps_busy[id(pb)] = ev
        pt = PTs[ti % 2]
        tr = None
        for kc in range(NKC):
            tr = b.emit("tensor", lambda e, kc=kc, pt=pt, xnb_t=xnb_t:
                        e.transpose(
                            pt[:, kc * 128:(kc + 1) * 128],
                            xnb_t[:, kc * 128:(kc + 1) * 128], identb[:]),
                        waits=[xnbo, pt_busy.get(id(pt))], inc="ts")
        xnb_free[pi2] = ("ts", b.cnt["ts"])
        ev = b.emit("vector", lambda e, ti=ti, pt=pt: e.tensor_copy(
            xnT[:].rearrange("p (kc t) -> p kc t", kc=NKC)
            [:, :, ti * 128:(ti + 1) * 128], pt[:]),
            waits=[tr], inc="vs")
        pt_busy[id(pt)] = ev
        par_done[pi2] = pv[0]
        if ti + 2 < NTI:
            ln_loads.append(b.dma(
                "gpsimd", xs[pi2],
                dx.rearrange("(ti p) c -> ti p c", p=128)[ti + 2],
                waits=[("ts", b.cnt["ts"])], chain="x%d" % (ti % 2)))
    p1_done = b.snap()
    if debug:
        b.dma("gpsimd", dbg["dbg_acc_p1"][:, :], acc[:], waits=p1_done)
        b.dma("gpsimd", dbg["dbg_xnT"][:, :], xnT[:], waits=p1_done)
        b.dma("gpsimd", dbg["dbg_stat"][:, :], stat[:, :], waits=p1_done)

    sp_busy = {}
    avp_busy = None
    wo_busy = {}
    wslice_free = {"qkv": None, "o": None}
    wcast_prev = None
    last_acc_add = None
    epi_mark = [None]
    lsc_prev = [None]
    zps_busy_dummy = {}

    def stream_wslice(ch, first):
        nonlocal wcast_prev
        outs = {}
        for (nm, wsb, dwt, kind) in [("q", wqs, dwq, "col"),
                                     ("k", wks, dwk, "col"),
                                     ("v", wvs, dwv, "col"),
                                     ("o", wos, dwo, "row")]:
            waits = [wslice_free["o" if nm == "o" else "qkv"]]
            if first:
                waits.append(consts_done)
            if kind == "col":
                src_ = dwt.rearrange("(kc p) c -> p kc c", p=128)[
                    :, :, ch * 128:(ch + 1) * 128]
                dst = wsb[:].rearrange("p (kc c) -> p kc c", kc=NKC)
            else:
                src_ = dwt.rearrange("(c p) m -> p c m", p=128)[:, ch, :]
                dst = wsb[:, 0:1024]
            outs[nm] = b.dma("gpsimd", dst, src_, waits=waits,
                             chain="ws_" + nm)
        wcast_prev = outs
        return outs

    for ch in range(NKC):
        wst = stream_wslice(ch, ch == 0)
        projp = [P[3], P[4]]
        pe_busy2 = {}
        qk_last = None
        rope_prev = {}
        for (wsb, dstb, wkey) in [(wqs, qb, "q"), (wks, kb, "k")]:
            for tt in range(NTT):
                pb = projp[tt % 2]
                w = p1_done + [wst[wkey], pe_busy2.get(id(pb)),
                               sp_busy.get(id(pb)), wo_busy.get(id(pb)),
                               ps_busy.get(id(pb))]
                mm = None
                for kc in range(NKC):
                    mm = b.emit("tensor", lambda e, kc=kc, tt=tt, wsb=wsb,
                                pb=pb: e.matmul(
                                    pb[:], wsb[:, kc * 128:(kc + 1) * 128],
                                    xnT[:, kc * T + tt * TT:
                                        kc * T + tt * TT + TT],
                                    start=(kc == 0), stop=(kc == NKC - 1)),
                                waits=w, inc="ts")
                sl = (tt % 2) * TT
                ev = b.emit("vector", lambda e, pb=pb, sl=sl: e.tensor_copy(
                    qtmp[:, sl:sl + TT], pb[:]), waits=[mm, epi_mark[0]],
                    hard=[rope_prev.get(sl)], inc="vs")
                rot = b.emit("tensor", lambda e, sl=sl, pb=pb: e.matmul(
                    pb[:], p128[:], qtmp[:, sl:sl + TT], start=True,
                    stop=True), waits=[ev], inc="ts")
                qc = b.emit("vector", lambda e, sl=sl, tt=tt, dstb=dstb:
                            e.tensor_tensor(
                                dstb[:, tt * TT:(tt + 1) * TT],
                                qtmp[:, sl:sl + TT],
                                cosb[:, tt * TT:(tt + 1) * TT], ALU.mult),
                            hard=[ev], inc="vs")
                rs = b.emit("vector", lambda e, sl=sl, tt=tt, pb=pb:
                            e.tensor_tensor(
                                qtmp[:, sl:sl + TT], pb[:],
                                sinb[:, tt * TT:(tt + 1) * TT], ALU.mult),
                            waits=[rot], hard=[qc], inc="vs")
                fin = b.emit("vector", lambda e, sl=sl, tt=tt, dstb=dstb:
                             e.tensor_tensor(
                                 dstb[:, tt * TT:(tt + 1) * TT],
                                 dstb[:, tt * TT:(tt + 1) * TT],
                                 qtmp[:, sl:sl + TT], ALU.add),
                             hard=[rs, qc], inc="vs")
                pe_busy2[id(pb)] = rs
                rope_prev[sl] = fin
                qk_last = fin
        v_done = None
        for tig in range(4):
            pb = projp[tig % 2]
            w = p1_done + [wst["v"], pe_busy2.get(id(pb))]
            mm = None
            for tl in range(4):
                ti = tig * 4 + tl
                for kc in range(NKC):
                    mm = b.emit("tensor", lambda e, kc=kc, ti=ti, tl=tl,
                                pb=pb: e.matmul(
                                    pb[:, tl * 128:(tl + 1) * 128],
                                    xnT[:, kc * T + ti * 128:
                                        kc * T + ti * 128 + 128],
                                    wvs[:, kc * 128:(kc + 1) * 128],
                                    start=(kc == 0), stop=(kc == NKC - 1)),
                                waits=w, inc="ts")
            ev = b.emit("vector", lambda e, tig=tig, pb=pb: e.tensor_copy(
                vb[:, tig * 512:(tig + 1) * 512], pb[:]), waits=[mm],
                inc="vs")
            pe_busy2[id(pb)] = ev
            v_done = ev
        wslice_free["qkv"] = ("ts", b.cnt["ts"])
        nm_prev = {}
        transp_end = {0: None, 1: None}
        prev_i = None

        def emit_transp_av(pi):
            nonlocal avp_busy, mmv_last
            mmv = None
            for h in range(2):
                hoff = h * T
                nb = pi + 1
                aev_g = {}
                tr = None
                for b0 in range(0, nb, 8):
                    bn = min(8, nb - b0)
                    pt = PTs[(b0 // 8) % 2]
                    tr = None
                    Spi = S_p0 if (pi % 2 == 0) else S_p1
                    for bb in range(bn):
                        blk = b0 + bb
                        tr = b.emit(
                            "tensor",
                            lambda e, hoff=hoff, blk=blk, bb=bb, pt=pt,
                            Spi=Spi: e.transpose(
                                pt[:, bb * 128:(bb + 1) * 128],
                                Spi[:, hoff + blk * 128:
                                    hoff + blk * 128 + 128], identb[:]),
                            waits=[nm_prev[h], pt_busy.get(id(pt))],
                            inc="ts")
                    aev = b.emit("vector",
                                 lambda e, b0=b0, bn=bn, pt=pt: e.tensor_copy(
                                     AT_sb[:, b0 * 128:(b0 + bn) * 128],
                                     pt[:, 0:bn * 128]),
                                 waits=[tr], inc="vs")
                    pt_busy[id(pt)] = aev
                    aev_g[b0 // 8] = aev
                transp_end[h] = ("ts", b.cnt["ts"])
                for blk in range(pi + 1):
                    wts = [aev_g[blk // 8], v_done]
                    if blk == 0 and h == 0:
                        wts.append(avp_busy)
                    mmv = b.emit(
                        "tensor",
                        lambda e, h=h, blk=blk, pi=pi: e.matmul(
                            P[5][h * 64:(h + 1) * 64, 0:128],
                            vb[:, blk * 128 + h * 64:
                               blk * 128 + h * 64 + 64],
                            AT_sb[:, blk * 128:(blk + 1) * 128],
                            start=(blk == 0), stop=(blk == pi),
                            tile_position=(0, h * 64)),
                        waits=wts, inc="ts")
            oev = b.emit("scalar", lambda e, pi=pi: e.copy(
                oT[:, pi * 128:(pi + 1) * 128], P[5][:, 0:128]),
                waits=[mmv, lsc_prev[0]], inc="ss")
            avp_busy = oev

        mmv_last = None

        def emit_smm_evacs(i, ex_mark):
            W = (i + 1) * 128
            nspan = (i // 4) + 1
            Ssb = S_p0 if (i % 2 == 0) else S_p1
            bankA = P[i % 2]
            bankB = P[2 + (i % 2)]
            for si in range(nspan):
                s0 = si * 512
                wdt = min(512, W - s0)
                mmAB = {}
                for (h, base, pb) in [(0, 0, bankA), (1, 64, bankB)]:
                    wts = [qk_last, sp_busy.get(id(pb)),
                           pe_busy2.get(id(pb)), wo_busy.get(id(pb))]
                    mmAB[h] = b.emit(
                        "tensor",
                        lambda e, i=i, s0=s0, wdt=wdt, base=base, pb=pb:
                        e.matmul(pb[:, 0:wdt],
                                 qb[base:base + 64, i * 128:(i + 1) * 128],
                                 kb[base:base + 64, s0:s0 + wdt],
                                 start=True, stop=True),
                        waits=wts, inc="ts")
                for (h, pb) in [(0, bankA), (1, bankB)]:
                    hoff = h * T
                    if si == nspan - 1:
                        r = i % 4
                        off = 384 - r * 128
                        ev = b.emit(
                            "vector",
                            lambda e, hoff=hoff, s0=s0, wdt=wdt, off=off,
                            pb=pb, Ssb=Ssb: e.scalar_tensor_tensor(
                                Ssb[:, hoff + s0: hoff + s0 + wdt],
                                pb[:, 0:wdt], -0.125,
                                maskm[:, off:off + wdt],
                                ALU.mult, ALU.add),
                            waits=[mmAB[h]] + list(ex_mark or []), inc="vs")
                    else:
                        ev = b.emit(
                            "scalar",
                            lambda e, hoff=hoff, s0=s0, wdt=wdt, pb=pb,
                            Ssb=Ssb: e.activation(
                                Ssb[:, hoff + s0: hoff + s0 + wdt],
                                pb[:, 0:wdt], AF.Copy, scale=-0.125),
                            waits=[mmAB[h]] + list(ex_mark or []), inc="ss")
                    sp_busy[id(pb)] = ev
                    if si == nspan - 1:
                        mask_marks.setdefault(i, []).append(ev)
            sev_marks[i] = ("ss", b.cnt["ss"])

        sev_marks = {}
        mask_marks = {}
        emit_smm_evacs(0, None)
        tr_m_prev = None
        for i in range(NTI):
            W = (i + 1) * 128
            if prev_i is not None:
                emit_transp_av(prev_i)
            Ssb_i = S_p0 if (i % 2 == 0) else S_p1
            ex = None
            for h in range(2):
                hoff = h * T
                lslot = (0 if h == 0 else 48) + i
                ex = b.emit(
                    "scalar", lambda e, hoff=hoff, W=W, h=h, lslot=lslot,
                    Ssb_i=Ssb_i: e.activation(
                        Ssb_i[:, hoff:hoff + W], Ssb_i[:, hoff:hoff + W],
                        AF.Exp, bias=0.0, scale=-1.0,
                        accum_out=stat[:, lslot:lslot + 1]),
                    waits=mask_marks[i] + [sev_marks.get(i)],
                    inc="ss")
                nm_prev[h] = ex
            if i + 1 < NTI:
                emit_smm_evacs(i + 1, tr_m_prev)
            tr_m_prev = [transp_end[0], transp_end[1]]
            prev_i = i
        emit_transp_av(prev_i)
        r1_ = b.emit("vector", lambda e: e.reciprocal(
            stat[:, 0:16], stat[:, 0:16]),
            waits=[nm_prev[0], nm_prev[1]], inc="vs")
        r2x = b.emit("vector", lambda e: e.reciprocal(
            stat[:, 48:64], stat[:, 48:64]), hard=[r1_], inc="vs")
        lt1 = b.emit("tensor", lambda e: e.transpose(
            P[4][0:16, 0:128], stat[:, 0:16], identf[:]),
            waits=[r2x, wo_busy.get(id(P[4])), pe_busy2.get(id(P[4])),
                   sp_busy.get(id(P[4]))], inc="ts")
        lt2 = b.emit("tensor", lambda e: e.transpose(
            P[5][0:16, 0:128], stat[:, 48:64], identf[:]),
            waits=[avp_busy, sp_busy.get(id(P[5]))], inc="ts")
        le1 = b.emit("vector", lambda e: e.tensor_copy(
            ltrb[0:16, 0:128], P[4][0:16, 0:128]), waits=[lt2], inc="vs")
        le2 = b.emit("vector", lambda e: e.tensor_copy(
            ltrb[0:16, 128:256], P[5][0:16, 0:128]), hard=[le1], inc="vs")
        lev = None
        for ti in range(NTI):
            pb = [P[0], P[1], P[2], P[3]][ti % 4]
            lmm = None
            for h in range(2):
                lmm = b.emit(
                    "tensor", lambda e, ti=ti, h=h, pb=pb: e.matmul(
                        pb[h * 64:(h + 1) * 64, 0:128],
                        sel16[0:16, ti * 128:ti * 128 + 64],
                        ltrb[0:16, h * 128:(h + 1) * 128],
                        start=True, stop=True, tile_position=(0, h * 64)),
                    waits=[le2, consts_done, sp_busy.get(id(pb))], inc="ts")
            lev = b.emit("vector", lambda e, ti=ti, pb=pb: e.tensor_copy(
                lbc[:, ti * 128:(ti + 1) * 128], pb[:, 0:128]),
                waits=[lmm], inc="vs")
            sp_busy[id(pb)] = lev
        epi_mark[0] = ("ts", b.cnt["ts"])
        lsc = b.emit("vector", lambda e: e.tensor_tensor(
            oT[:], oT[:], lbc[:], ALU.mult),
            waits=[("ss", b.cnt["ss"])], hard=[lev], inc="vs")
        lsc_prev[0] = lsc
        wop = [P[3], P[4]]
        for mc in range(NKC):
            for tt in range(NTT):
                pb = wop[(mc * NTT + tt) % 2]
                wts = [avp_busy, wst["o"], wo_busy.get(id(pb)),
                       pe_busy2.get(id(pb)), sp_busy.get(id(pb)), lsc]
                mm = b.emit("tensor", lambda e, mc=mc, tt=tt, pb=pb: e.matmul(
                    pb[:], wos[:, mc * 128:(mc + 1) * 128],
                    oT[:, tt * TT:(tt + 1) * TT], start=True, stop=True),
                    waits=wts, inc="ts")
                ad = b.emit("vector", lambda e, mc=mc, tt=tt, pb=pb:
                            e.tensor_tensor(
                                acc[:, mc * T + tt * TT:
                                    mc * T + tt * TT + TT],
                                acc[:, mc * T + tt * TT:
                                    mc * T + tt * TT + TT],
                                pb[:], ALU.add),
                            waits=[mm], inc="vs")
                wo_busy[id(pb)] = ad
                last_acc_add = ad
        wslice_free["o"] = ("ts", b.cnt["ts"])
    attn_done = b.snap()
    if debug:
        b.dma("gpsimd", dbg["dbg_acc_p2"][:, :], acc[:], waits=attn_done)
        b.dma("gpsimd", dbg["dbg_qb"][:, :], qb[:], waits=attn_done)
        b.dma("gpsimd", dbg["dbg_kb"][:, :], kb[:], waits=attn_done)
        b.dma("gpsimd", dbg["dbg_vb"][:, :], vb[:], waits=attn_done)
        b.dma("gpsimd", dbg["dbg_oT"][:, :], oT[:], waits=attn_done)

    trmA = b.emit("tensor", lambda e: e.transpose(
        P[4][0:16, 0:128], stat[:, 16:32], identf[:]),
        waits=attn_done + [ps_busy.get(id(P[4])), wo_busy.get(id(P[4]))],
        inc="ts")
    trmB = b.emit("tensor", lambda e: e.transpose(
        P[5][0:16, 0:128], stat[:, 32:48], identf[:]),
        waits=[ps_busy.get(id(P[5])), wo_busy.get(id(P[5]))], inc="ts")
    tevA = b.emit("vector", lambda e: e.tensor_copy(
        trscb[0:16, 0:128], P[4][0:16, 0:128]), waits=[trmA], inc="vs")
    tevB = b.emit("vector", lambda e: e.tensor_copy(
        trscb[0:16, 128:256], P[5][0:16, 0:128]), waits=[trmB], inc="vs")
    ps_busy[id(P[4])] = tevA
    ps_busy[id(P[5])] = tevB
    bc_last = None
    for ti in range(NTI):
        pb = P[4 + (ti % 2)]
        mmM = b.emit("tensor", lambda e, ti=ti, pb=pb: e.matmul(
            pb[:, 0:128], sel16[0:16, ti * 128:(ti + 1) * 128],
            trscb[0:16, 0:128], start=True, stop=True),
            waits=[tevA, tevB, consts_done, ps_busy.get(id(pb)),
                   wo_busy.get(id(pb))], inc="ts")
        mmR = b.emit("tensor", lambda e, ti=ti, pb=pb: e.matmul(
            pb[:, 128:256], sel16[0:16, ti * 128:(ti + 1) * 128],
            trscb[0:16, 128:256], start=True, stop=True), inc="ts")
        evM = b.emit("vector", lambda e, ti=ti, pb=pb: e.tensor_copy(
            mu2bc[:, ti * 128:(ti + 1) * 128], pb[:, 0:128]),
            waits=[mmM], inc="vs")
        evR = b.emit("vector", lambda e, ti=ti, pb=pb: e.tensor_copy(
            r2bc[:, ti * 128:(ti + 1) * 128], pb[:, 128:256]),
            waits=[mmR], inc="vs")
        ps_busy[id(pb)] = evR
        bc_last = evR
    xh_done = []
    tprev = None
    for kc in range(NKC):
        t1 = b.emit("vector", lambda e, kc=kc: e.tensor_tensor(
            xh_tmp[:], xnT[:, kc * T:(kc + 1) * T], mu2bc[:], ALU.subtract),
            waits=[bc_last], hard=[tprev], inc="vs")
        t2 = b.emit("vector", lambda e, kc=kc: e.tensor_tensor(
            xhatT[:, kc * T:(kc + 1) * T], xh_tmp[:], r2bc[:], ALU.mult),
            hard=[t1], inc="vs")
        tprev = t2
        xh_done.append(t2)
    ccast = b.dma("gpsimd", centb[:], dcent[:, :], waits=attn_done,
                  chain="cs")
    trc = None
    for kc in range(NKC):
        trc = b.emit("tensor", lambda e, kc=kc: e.transpose(
            PT0[:, kc * 8:(kc + 1) * 8], centb[:, kc * 128:(kc + 1) * 128],
            identb[0:8, 0:8]),
            waits=[ccast, pt_busy.get(id(PT0))], inc="ts")
    cev = b.emit("vector", lambda e: e.tensor_copy(centT[:], PT0[:, 0:64]),
                 waits=[trc], inc="vs")
    pt_busy[id(PT0)] = cev
    aff_last = None
    for tt in range(NTT):
        pb = P[4 + (tt % 2)]
        mm = None
        for kc in range(NKC):
            mm = b.emit("tensor", lambda e, kc=kc, tt=tt, pb=pb: e.matmul(
                pb[0:8, :], centT[:, kc * 8:(kc + 1) * 8],
                xnT[:, kc * T + tt * TT: kc * T + tt * TT + TT],
                start=(kc == 0), stop=(kc == NKC - 1)),
                waits=[cev, ps_busy.get(id(pb))], inc="ts")
        ev = b.emit("vector", lambda e, tt=tt, pb=pb: e.tensor_copy(
            affem[:, tt * TT:(tt + 1) * TT], pb[0:8, :]), waits=[mm],
            inc="vs")
        ps_busy[id(pb)] = ev
        aff_last = ev
    atrd = None
    for g in range(2):
        pb = P[4 + g]
        tr = None
        for j in range(8):
            ti = g * 8 + j
            tr = b.emit("tensor", lambda e, ti=ti, j=j, pb=pb: e.transpose(
                pb[:, j * 8:(j + 1) * 8], affem[:, ti * 128:(ti + 1) * 128],
                identf[0:8, 0:8]),
                waits=[aff_last, ps_busy.get(id(pb))], inc="ts")
        ev = b.emit("vector", lambda e, g=g, pb=pb: e.tensor_copy(
            afftm[:, g * 64:(g + 1) * 64], pb[:, 0:64]), waits=[tr], inc="vs")
        ps_busy[id(pb)] = ev
        atrd = ev
    pvc = [None]

    def vch2(fn, waits=()):
        pvc[0] = b.emit("vector", fn, waits=waits, hard=[pvc[0]], inc="vs")
        return pvc[0]

    mc_ = vch2(lambda e: e.tensor_reduce(
        stat[:, 48:64], afftm[:].rearrange("p (ti e) -> p ti e", e=8),
        AX.X, ALU.max), waits=[atrd])
    for ti in range(NTI):
        vch2(lambda e, ti=ti: e.tensor_scalar(
            ismax[:, ti * 8:(ti + 1) * 8], afftm[:, ti * 8:(ti + 1) * 8],
            stat[:, 48 + ti:49 + ti], None, ALU.is_ge))
    for ti in range(NTI):
        vch2(lambda e, ti=ti: e.tensor_tensor(
            ismax[:, ti * 8:(ti + 1) * 8], ismax[:, ti * 8:(ti + 1) * 8],
            c916[:], ALU.mult), waits=[consts_done])
    vch2(lambda e: e.tensor_scalar(
        ismax[:], ismax[:], -1.0, 9.0, ALU.mult, ALU.add))
    em_ = vch2(lambda e: e.tensor_reduce(
        stat[:, 64:80], ismax[:].rearrange("p (ti e) -> p ti e", e=8),
        AX.X, ALU.min))
    msk_last = None
    for ti in range(NTI):
        msk_last = vch2(lambda e, ti=ti: e.tensor_scalar(
            ismax[:, ti * 8:(ti + 1) * 8], iota8[:],
            stat[:, 64 + ti:65 + ti], None, ALU.is_equal))
    alp = b.emit("scalar", lambda e: e.activation(
        affem[:], affem[:], AF.Sigmoid),
        waits=[atrd], inc="ss")
    mtr = None
    for g in range(4):
        pb = P[4 + (g % 2)]
        tr = None
        for j in range(4):
            ti = g * 4 + j
            tr = b.emit("tensor", lambda e, ti=ti, j=j, pb=pb: e.transpose(
                pb[0:8, j * 128:(j + 1) * 128],
                ismax[:, ti * 8:(ti + 1) * 8], identf[:]),
                waits=[msk_last, ps_busy.get(id(pb))], inc="ts")
        ev = b.emit("vector", lambda e, g=g, pb=pb: e.tensor_copy(
            maskem[:, g * 512:(g + 1) * 512], pb[0:8, 0:512]),
            waits=[tr], inc="vs")
        ps_busy[id(pb)] = ev
        mtr = ev
    wem_done = b.emit("vector", lambda e: e.tensor_tensor(
        w_em[:], maskem[:], affem[:], ALU.mult), waits=[alp],
        hard=[mtr], inc="vs")
    selc = b.dma("sync", selb[:], dc["c_sel"][:, :], waits=attn_done,
                 chain="sel")
    p25_done = b.snap()
    if debug:
        b.dma("gpsimd", dbg["dbg_xhatT"][:, :], xhatT[:], waits=p25_done)
        b.dma("gpsimd", dbg["dbg_wem"][:, :], w_em[:], waits=p25_done)
        b.dma("gpsimd", dbg["dbg_maskem"][:, :], maskem[:], waits=p25_done)
        b.dma("gpsimd", dbg["dbg_alpha"][:, :], affem[:], waits=p25_done)

    w1r = dw1.rearrange("e (kc p) c -> e kc p c", p=128)
    w2r = dw2.rearrange("e (fc p) c -> e fc p c", p=128)
    TPS = [P[0], P[1]]
    ZPS = [P[2], P[3]]
    tps_busy = {}
    zps_busy = {}
    w1qs, w2qs = [w1q0, w1q1], [w2q0, w2q1]
    hbufs = [hbuf0, hbuf1]
    b1ts, gamTs, betTs = [b1t0, b1t1], [gamT0, gamT1], [betT0, betT1]
    pb1s, pgms, pbts = [pb1a, pb1b], [pgma, pgmb], [pbta, pbtb]
    QF = 8
    NSTEP = NE * 4
    NGT = NSTEP * 4

    def stream_w1_q(e_, q, buf, extra_waits):
        return b.dma("gpsimd",
                     w1qs[buf][:].rearrange("p (kc c) -> p kc c", kc=NKC),
                     w1r[e_, :, :, q * 1024:(q + 1) * 1024]
                     .rearrange("kc p c -> p kc c"),
                     waits=extra_waits, chain="w1")

    def stream_w2_q(e_, q, buf, extra_waits):
        return b.dma("gpsimd",
                     w2qs[buf][:].rearrange("p (fc c) -> p fc c", fc=QF),
                     w2r[e_, q * QF:(q + 1) * QF]
                     .rearrange("fc p c -> p fc c"),
                     waits=extra_waits, chain="w2")

    pstate = {}

    def param_dmas(e_, extra_waits):
        pi = e_ % 2
        prev = pstate.get(e_ - 2, {})
        st = {}
        st["l1"] = b.dma("gpsimd", pb1s[pi][:],
                         db1.rearrange("e (a c) -> e a c", c=128)[e_],
                         waits=list(extra_waits) + [prev.get("tb1")],
                         chain="pp")
        st["lg"] = b.dma("gpsimd", pgms[pi][:],
                         dgam.rearrange("e (a c) -> e a c", c=128)[e_],
                         waits=[prev.get("tgb")], chain="pp")
        st["lb"] = b.dma("gpsimd", pbts[pi][:],
                         dbet.rearrange("e (a c) -> e a c", c=128)[e_],
                         waits=[prev.get("tgb")], chain="pp")
        pstate[e_] = st

    def param_compute(e_, extra_waits):
        pi = e_ % 2
        st = pstate[e_]
        prev = pstate.get(e_ - 2, {})
        t1 = b.emit("tensor", lambda e, pi=pi: e.transpose(
            P[4][:, 0:32], pb1s[pi][:], identf[0:32, 0:32]),
            waits=list(extra_waits) + [st["l1"], ps_busy.get(id(P[4]))],
            inc="ts")
        st["tb1"] = ("ts", b.cnt["ts"])
        e1 = b.emit("vector", lambda e, pi=pi: e.tensor_copy(
            b1ts[pi][:], P[4][:, 0:32]),
            waits=[t1, prev.get("rl_last")], inc="vs")
        ps_busy[id(P[4])] = e1
        st["b1"] = e1
        t3 = b.emit("tensor", lambda e, pi=pi: e.transpose(
            P[5][:, 0:8], pgms[pi][:], identf[0:8, 0:8]),
            waits=[st["lg"], ps_busy.get(id(P[5]))], inc="ts")
        t4 = b.emit("tensor", lambda e, pi=pi: e.transpose(
            P[5][:, 8:16], pbts[pi][:], identf[0:8, 0:8]),
            waits=[st["lb"]], inc="ts")
        st["tgb"] = ("ts", b.cnt["ts"])
        b.emit("vector", lambda e, pi=pi: e.tensor_copy(
            gamTs[pi][:], P[5][:, 0:8]), waits=[t4], inc="vs")
        e4 = b.emit("vector", lambda e, pi=pi: e.tensor_copy(
            betTs[pi][:], P[5][:, 8:16]), inc="vs")
        ps_busy[id(P[5])] = e4
        st["gb"] = e4

    def emit_wbc(e_):
        wbc_done = None
        for tt in range(NTT):
            pb = P[4] if (tt % 2 == 0) else P[5]
            mm = b.emit("tensor", lambda e, e_=e_, tt=tt, pb=pb: e.matmul(
                pb[:], selb[:, e_ * 128:(e_ + 1) * 128],
                w_em[:, tt * TT:(tt + 1) * TT], start=True, stop=True),
                waits=[selc, wem_done, ps_busy.get(id(pb)),
                       zps_busy.get(id(pb))], inc="ts")
            ev = b.emit("vector", lambda e, tt=tt, pb=pb: e.tensor_copy(
                wbcb[:, tt * TT:(tt + 1) * TT], pb[:]),
                waits=[mm], inc="vs")
            ps_busy[id(pb)] = ev
            wbc_done = ev
        pstate[e_]["wbc"] = wbc_done

    zlast = [None]
    hbuf_busy = [None, None]
    tbuf_busy = [None]
    w1q_busy = [None, None]
    w2q_busy = [None, None]
    hd_done = {}

    def emit_hd(gti, extra=()):
        e_ = gti // 16
        tt = gti % 4
        pi2 = e_ % 2
        hb = hbufs[gti % 2]
        last = None
        for kc in range(NKC):
            last = b.emit(
                "vector",
                lambda e, kc=kc, tt=tt, pi2=pi2, hb=hb: e.tensor_scalar(
                    hb[:, kc * TT:(kc + 1) * TT],
                    xhatT[:, kc * T + tt * TT: kc * T + tt * TT + TT],
                    gamTs[pi2][:, kc:kc + 1], betTs[pi2][:, kc:kc + 1],
                    ALU.mult, ALU.add),
                waits=list(extra) + [xh_done[kc], hbuf_busy[gti % 2]],
                inc="vs")
        hd_done[gti] = last

    w1_ready = [None, None]
    w2_ready = [None, None]
    w1_ready[0] = stream_w1_q(0, 0, 0, p25_done)
    w2_ready[0] = stream_w2_q(0, 0, 0, p25_done)
    w1_ready[1] = stream_w1_q(0, 1, 1, p25_done)
    w2_ready[1] = stream_w2_q(0, 1, 1, p25_done)
    param_dmas(0, p25_done)
    param_compute(0, p25_done)
    emit_wbc(0)
    param_dmas(1, [])
    b2ld = b.dma("gpsimd", b2all[:], db2[:, :], waits=p25_done, chain="pp")
    emit_hd(0, p25_done)

    for s in range(NSTEP):
        e_, q = divmod(s, 4)
        buf = s % 2
        pi = e_ % 2
        est = pstate[e_]
        for tt in range(NTT):
            gti = s * 4 + tt
            hb = hbufs[gti % 2]
            tb = tbufq
            relu_ms = {}
            for fcl in range(QF):
                fc = q * QF + fcl
                bank = TPS[fcl % 2]
                t_mm_last = None
                for kc in range(NKC):
                    wts = [hd_done[gti], w1_ready[buf]]
                    if kc == 0:
                        wts.append(tps_busy.get(id(bank)))
                    t_mm_last = b.emit(
                        "tensor",
                        lambda e, kc=kc, fcl=fcl, bank=bank, buf=buf, hb=hb:
                        e.matmul(
                            bank[:],
                            w1qs[buf][:].rearrange("p (kc c) -> p kc c",
                                                   kc=NKC)
                            [:, kc, fcl * 128:(fcl + 1) * 128],
                            hb[:, kc * TT:(kc + 1) * TT],
                            start=(kc == 0), stop=(kc == NKC - 1)),
                        waits=wts, inc="ts")
                wrl = [t_mm_last, est["b1"], tbuf_busy[0]]
                if s == 0 and tt == 0 and fcl == 0:
                    wrl += p25_done
                rl = b.emit(
                    "scalar",
                    lambda e, fcl=fcl, fc=fc, bank=bank, pi=pi, tb=tb:
                    e.activation(
                        tb[:, fcl * TT:(fcl + 1) * TT], bank[:],
                        AF.Relu, bias=b1ts[pi][:, fc:fc + 1], scale=1.0),
                    waits=wrl, inc="ss")
                tps_busy[id(bank)] = rl
                relu_ms[fcl] = rl
            hbuf_busy[gti % 2] = ("ts", b.cnt["ts"])
            if gti + 1 < NGT:
                emit_hd(gti + 1)
            for mc in range(NKC):
                bank = ZPS[mc % 2]
                z_mm_last = None
                for fcl in range(QF):
                    wts = [relu_ms[fcl], w2_ready[buf]]
                    if fcl == 0:
                        wts.append(zps_busy.get(id(bank)))
                    z_mm_last = b.emit(
                        "tensor",
                        lambda e, fcl=fcl, mc=mc, bank=bank, buf=buf, tb=tb:
                        e.matmul(
                            bank[:],
                            w2qs[buf][:].rearrange("p (fc c) -> p fc c",
                                                   fc=QF)
                            [:, fcl, mc * 128:(mc + 1) * 128],
                            tb[:, fcl * TT:(fcl + 1) * TT],
                            start=(fcl == 0),
                            stop=(fcl == QF - 1),
                            skip_group_check=True),
                        waits=wts, inc="ts")
                zm = b.emit(
                    "vector",
                    lambda e, tt=tt, bank=bank: e.tensor_tensor(
                        ztmp[:], bank[:],
                        wbcb[:, tt * TT:(tt + 1) * TT], ALU.mult),
                    waits=[z_mm_last],
                    hard=[est["wbc"], zlast[0]], inc="vs")
                zps_busy[id(bank)] = zm
                zlast[0] = b.emit(
                    "vector",
                    lambda e, mc=mc, tt=tt: e.tensor_tensor(
                        acc[:, mc * T + tt * TT: mc * T + tt * TT + TT],
                        acc[:, mc * T + tt * TT: mc * T + tt * TT + TT],
                        ztmp[:], ALU.add),
                    waits=[last_acc_add], hard=[zm], inc="vs")
            tbuf_busy[0] = ("ts", b.cnt["ts"])
        w1q_busy[buf] = ("ts", b.cnt["ts"])
        w2q_busy[buf] = ("ts", b.cnt["ts"])
        if s + 2 < NSTEP:
            e2_, q2 = divmod(s + 2, 4)
            w1_ready[buf] = stream_w1_q(e2_, q2, buf, [w1q_busy[buf]])
            w2_ready[buf] = stream_w2_q(e2_, q2, buf, [w2q_busy[buf]])
        if q == 0 and 2 <= e_ + 1 < NE:
            param_dmas(e_ + 1, [])
        if q == 2 and e_ + 1 < NE:
            param_compute(e_ + 1, [])
        if q == 3:
            pstate[e_]["rl_last"] = ("ss", b.cnt["ss"])
            if e_ + 1 < NE:
                emit_wbc(e_ + 1)
    for tt in range(NTT):
        for mc in range(NKC):
            bank = ZPS[mc % 2]
            mm = b.emit(
                "tensor",
                lambda e, mc=mc, tt=tt, bank=bank: e.matmul(
                    bank[:], b2all[:, mc * 128:(mc + 1) * 128],
                    w_em[:, tt * TT:(tt + 1) * TT], start=True, stop=True),
                waits=[b2ld, wem_done, zps_busy.get(id(bank))], inc="ts")
            ad = b.emit(
                "vector",
                lambda e, mc=mc, tt=tt, bank=bank: e.tensor_tensor(
                    acc[:, mc * T + tt * TT: mc * T + tt * TT + TT],
                    acc[:, mc * T + tt * TT: mc * T + tt * TT + TT],
                    bank[:], ALU.add),
                waits=[mm], inc="vs")
            zps_busy[id(bank)] = ad
            zlast[0] = ad
    ffn_done = b.snap()

    outb = [out_tm0, out_tm1]
    ost_busy = [None, None]
    final_dma = None
    last_store = [None, None]
    for ti in range(NTI):
        ob = outb[ti % 2]
        evs = None
        for half in range(2):
            pb = P[4 + half]
            tr = None
            for j in range(4):
                kc = half * 4 + j
                tr = b.emit("tensor", lambda e, kc=kc, j=j, ti=ti, pb=pb:
                            e.transpose(
                                pb[:, j * 128:(j + 1) * 128],
                                acc[:, kc * T + ti * 128:
                                    kc * T + ti * 128 + 128], identf[:]),
                            waits=ffn_done + [ps_busy.get(id(pb))],
                            inc="ts")
            ev = b.emit("vector", lambda e, half=half, ob=ob, pb=pb:
                        e.tensor_copy(
                            ob[:, half * 512:(half + 1) * 512], pb[:]),
                        waits=[tr, ost_busy[ti % 2]], inc="vs")
            ps_busy[id(pb)] = ev
            evs = ev
        st = b.dma("gpsimd", dout.rearrange("(ti p) c -> ti p c", p=128)[ti],
                   ob[:], waits=[evs], chain="o%d" % (ti % 2))
        ost_busy[ti % 2] = st
        final_dma = st
        last_store[ti % 2] = st
    b.emit("gpsimd", lambda e: e.nop(),
           waits=[last_store[0], last_store[1]])

    with nc.Block() as block:
        semstack = b.finalize(block)
    semstack.close()
    ovstack.close()
    al.close()
    return nc


def kernel(L0_attn_x, L0_attn_Wq, L0_attn_Wk, L0_attn_Wv, L0_attn_Wo,
           L0_ffn_centroids, L0_ffn_w1, L0_ffn_b1, L0_ffn_w2, L0_ffn_b2,
           L0_ffn_gamma, L0_ffn_beta):
    x = np.ascontiguousarray(np.asarray(L0_attn_x, dtype=np.float32))
    assert x.shape == (8, T, D)
    shared = {
        "L0_attn_Wq": np.ascontiguousarray(L0_attn_Wq, dtype=np.float32),
        "L0_attn_Wk": np.ascontiguousarray(L0_attn_Wk, dtype=np.float32),
        "L0_attn_Wv": np.ascontiguousarray(L0_attn_Wv, dtype=np.float32),
        "L0_attn_Wo": np.ascontiguousarray(L0_attn_Wo, dtype=np.float32),
        "L0_ffn_centroids": np.ascontiguousarray(L0_ffn_centroids,
                                                 dtype=np.float32),
        "L0_ffn_w1": np.ascontiguousarray(L0_ffn_w1, dtype=np.float32),
        "L0_ffn_b1": np.ascontiguousarray(L0_ffn_b1, dtype=np.float32),
        "L0_ffn_w2": np.ascontiguousarray(L0_ffn_w2, dtype=np.float32),
        "L0_ffn_b2": np.ascontiguousarray(L0_ffn_b2, dtype=np.float32),
        "L0_ffn_gamma": np.ascontiguousarray(L0_ffn_gamma, dtype=np.float32),
        "L0_ffn_beta": np.ascontiguousarray(L0_ffn_beta, dtype=np.float32),
    }
    shared.update(make_consts())
    nc = build_program()
    in_maps = []
    for c in range(8):
        m = dict(shared)
        m["L0_attn_x"] = np.ascontiguousarray(x[c])
        in_maps.append(m)
    res = run_bass_kernel_spmd(nc, in_maps, core_ids=list(range(8)))
    return np.stack([np.asarray(r["out"], dtype=np.float32)
                     for r in res.results], axis=0)
